# version 02 (final)
# baseline (speedup 1.0000x reference)
import numpy as np
import ml_dtypes
import contextlib
import concourse.bass as bass
import concourse.mybir as mybir
from concourse.bass_utils import run_bass_kernel_spmd

F32 = mybir.dt.float32
BF16 = mybir.dt.bfloat16
AF = mybir.ActivationFunctionType
ALU = mybir.AluOpType
AX = mybir.AxisListType
BF = ml_dtypes.bfloat16

T = 2048
D = 2048
NL = 512
NH = 16
HS = 128
HKV = 4
EXP = 8
FI = 1408
NW = 4
NC = 16
C_SHIFT = 24.0
ROPE_THETA = 1e6
ISQ = 1.0 / np.sqrt(128.0)

f32 = np.float32


class Emitter:
    ENGS = ("tensor", "vector", "scalar", "sync", "gpsimd")

    def __init__(self, nc):
        self.nc = nc
        self.ops = {e: [] for e in self.ENGS}
        self.cnt = {}
        self.sems = {}
        self.waited = {e: {} for e in self.ENGS}

    def sem(self, name, stack):
        h = stack.enter_context(self.nc.semaphore(name))
        self.sems[name] = h
        self.cnt[name] = 0

    def w(self, eng, sname, val):
        if val is None or val <= 0:
            return
        if self.waited[eng].get(sname, -1) >= val:
            return
        self.waited[eng][sname] = val
        h = self.sems[sname]
        self.ops[eng].append(lambda e, h=h, v=val: e.wait_ge(h, v))

    ENG_SEM = {"vector": "DV", "scalar": "SC"}

    def op(self, eng, fn, inc=None, n=1):
        s = self.ENG_SEM.get(eng)
        if s is not None:
            self.w(eng, s, self.cnt[s])
        if n > 1:
            for s2 in ("PE", "SC", "DV"):
                self.w(eng, s2, self.cnt[s2])
            if inc is not None:
                self.w(eng, inc, self.cnt[inc])
        if inc is None:
            self.ops[eng].append(lambda e, fn=fn: fn(e))
            return None
        h = self.sems[inc]
        self.cnt[inc] += n
        self.ops[eng].append(lambda e, fn=fn, h=h, n=n: fn(e).then_inc(h, n))
        return self.cnt[inc]

    def barrier(self):
        snap = dict(self.cnt)
        for e in self.ENGS:
            for s, v in snap.items():
                self.w(e, s, v)


def build_nc():
    nc = bass.Bass("TRN2", target_bir_lowering=False)
    dt = nc.dram_tensor

    xfm = dt("xfm", [D, T], F32, kind="ExternalInput").ap()
    wqn = dt("wqn", [D, D], F32, kind="ExternalInput").ap()
    wuk = dt("wuk", [D, NL], F32, kind="ExternalInput").ap()
    wuv = dt("wuv", [D, NL], F32, kind="ExternalInput").ap()
    woT = dt("woT", [D, D], F32, kind="ExternalInput").ap()
    wdkvT = dt("wdkvT", [D, NL], F32, kind="ExternalInput").ap()
    wqT = dt("wqT", [D, D], F32, kind="ExternalInput").ap()
    qwT = dt("qwT", [D, D], F32, kind="ExternalInput").ap()
    kwT = dt("kwT", [D, NL], F32, kind="ExternalInput").ap()
    vwT = dt("vwT", [D, NL], F32, kind="ExternalInput").ap()
    owT = dt("owT", [D, D], F32, kind="ExternalInput").ap()
    gwT = dt("gwT", [D, EXP], F32, kind="ExternalInput").ap()
    wgT = dt("wgT", [EXP, D, FI], F32, kind="ExternalInput").ap()
    wuT = dt("wuT", [EXP, D, FI], F32, kind="ExternalInput").ap()
    wdTh = dt("wdTh", [EXP, FI, 1024], F32, kind="ExternalInput").ap()
    emat_i = dt("emat", [D, 1024], BF16, kind="ExternalInput").ap()
    anw_i = dt("anw", [128, NC], F32, kind="ExternalInput").ap()
    fnw_i = dt("fnw", [128, NC], F32, kind="ExternalInput").ap()
    qnw_i = dt("qnw", [128, 1], F32, kind="ExternalInput").ap()
    knw_i = dt("knw", [128, 1], F32, kind="ExternalInput").ap()
    ct_i = dt("ct", [128, T], BF16, kind="ExternalInput").ap()
    st_i = dt("st", [128, T], BF16, kind="ExternalInput").ap()
    cm_i = dt("cm", [128, 896], BF16, kind="ExternalInput").ap()
    ones_i = dt("onesb", [128, 128], BF16, kind="ExternalInput").ap()
    onesf_i = dt("onesf", [1, 128], F32, kind="ExternalInput").ap()
    identb_i = dt("identb", [128, 128], BF16, kind="ExternalInput").ap()
    identf_i = dt("identf", [128, 128], F32, kind="ExternalInput").ap()

    outp = dt("outp", [1024, T], F32, kind="ExternalOutput").ap()

    keffd = dt("keffd", [NH, 128, NL], BF16)
    veffd = dt("veffd", [4, 128, D], BF16)
    qd = dt("qd", [NH, 128, T], BF16)
    x1d = dt("x1d", [NC, 128, T], BF16)
    y2d = dt("y2d", [NC, 128, T], BF16)
    g2d = dt("g2d", [NC, 128, T], BF16)
    actd = dt("actd", [EXP, 11, 128, T], BF16)

    with contextlib.ExitStack() as stack:
        En = stack.enter_context
        PB_N = 79104
        PF_N = 7488
        pb = En(nc.sbuf_tensor("pb", [128, PB_N], BF16))
        pf = En(nc.sbuf_tensor("pf", [128, PF_N], F32))
        oacc = En(nc.sbuf_tensor("oacc", [128, 2 * T], F32))
        ps = [En(nc.psum_tensor(f"ps{i}", [128, 512], F32)) for i in range(7)]
        pT = En(nc.psum_tensor("pT", [128, 1024], BF16))

        em = Emitter(nc)
        for s in ("LW", "LX", "PE", "DV", "SC", "ST"):
            em.sem(s, stack)
        blk = En(nc.Block())

        o_ct = 0
        o_st = 2048
        o_cm = 4096
        o_ones = 4992
        o_identb = 5120
        o_gw = 5248
        CB = 5376
        RB = CB
        RC = CB + 8192
        RD = CB + 16384
        RF = CB + 24576
        SMALL = RF + 32768
        o_probs = [SMALL, SMALL + 512]
        o_sq = SMALL + 1024
        SH = SMALL + 1536
        o_qlat = SH
        o_ylat = SH + 2048
        o_actc = SH
        o_cb = SH + 2048
        o_emq = SH
        o_rb = SMALL + 5632
        o_swp = SMALL + 6144
        o_gb = SMALL + 6656
        o_wb = [SMALL + 7168, SMALL + 9216]
        o_qh2 = [SMALL + 11264, SMALL + 13312]
        o_kpre = SMALL + 15360
        o_sg = SMALL + 15872
        PB_NEED = SMALL + 16384
        assert PB_NEED == PB_N, (PB_NEED, PB_N)

        of_anw = 0
        of_nanw = 16
        of_fnw = 32
        of_qnw = 48
        of_knw = 49
        of_onesf = 64
        of_identf = 192
        of_x0s = [320, 320]
        of_bc = 832
        of_tmp = 1856
        r_m2 = 2368
        r_a2 = 4416
        r_t1 = 6464
        r_t2 = 6976
        assert 7488 <= PF_N

        def bs(off, w):
            return pb[:, off:off + w]

        def fs(off, w):
            return pf[:, off:off + w]

        def row(off, w):
            return pf[0:1, off:off + w]

        for dst, src in [
            (bs(o_ct, T), ct_i), (bs(o_st, T), st_i), (bs(o_cm, 896), cm_i),
            (bs(o_ones, 128), ones_i), (bs(o_identb, 128), identb_i),
            (bs(o_gw, NC * EXP), gwT.rearrange("(c p) e -> p c e", p=128)),
            (fs(of_anw, NC), anw_i), (fs(of_fnw, NC), fnw_i),
            (fs(of_qnw, 1), qnw_i), (fs(of_knw, 1), knw_i),
            (pf[0:1, of_onesf:of_onesf + 128], onesf_i),
            (fs(of_identf, 128), identf_i),
        ]:
            em.op("gpsimd", lambda e, d=dst, s=src: e.dma_start(out=d, in_=s), inc="LW", n=16)
        lw_consts = em.cnt["LW"]
        em.w("vector", "LW", lw_consts)
        em.op("vector", lambda e: e.tensor_scalar(
            out=fs(of_nanw, NC), in0=fs(of_anw, NC), scalar1=-1.0, scalar2=None,
            op0=ALU.mult), inc="DV")
        em.w("tensor", "LW", lw_consts)

        def cast_load(dram_ap, off, width, eng="gpsimd"):
            em.w(eng, "PE", em.cnt["PE"])
            em.op(eng, lambda e, a=dram_ap, o=off, w=width: e.dma_start(
                out=bs(o, NC * w), in_=a.rearrange("(c p) n -> p c n", p=128)),
                inc="LW", n=16)
            return em.cnt["LW"]

        def mm(out, lhsT, rhs, start, stop, inc=None):
            if start:
                em.w("tensor", "SC", em.cnt["SC"])
                em.w("tensor", "DV", em.cnt["DV"])
            return em.op("tensor", lambda e, o=out, l=lhsT, r=rhs, s1=start, s2=stop:
                         e.matmul(o, l, r, start=s1, stop=s2), inc=inc)

        wb_free = [0, 0]

        def stream_cols(dram_ap, ncols, par):
            em.w("gpsimd", "PE", wb_free[par])
            em.op("gpsimd", lambda e, a=dram_ap, o=o_wb[par], w=ncols: e.dma_start(
                out=bs(o, NC * w), in_=a.rearrange("(c p) n -> p c n", p=128)),
                inc="LW", n=16)
            return em.cnt["LW"]

        lw_wuk = cast_load(wuk, RB, NL)
        for m in range(NC):
            par = m % 2
            lwv = stream_cols(wqn[:, m * 128:(m + 1) * 128], 128, par)
            em.w("tensor", "LW", max(lwv, lw_wuk))
            for j in range(NC):
                mm(ps[0][:, :], bs(o_wb[par] + j * 128, 128), bs(RB + j * NL, NL),
                   j == 0, j == NC - 1, inc="PE" if j == NC - 1 else None)
            wb_free[par] = em.cnt["PE"]
            em.w("scalar", "PE", em.cnt["PE"])
            em.op("scalar", lambda e: e.copy(out=bs(RD, NL), in_=ps[0][:, :]), inc="SC")
            sc_e = em.cnt["SC"]
            em.w("tensor", "SC", sc_e)
            em.w("sync", "SC", sc_e)
            em.op("sync", lambda e, m=m: e.dma_start(
                out=keffd[m], in_=bs(RD, NL)), inc="ST", n=16)
            em.w("scalar", "ST", em.cnt["ST"])
        lw_wuv = cast_load(wuv, RB, NL)
        wo_free = 0
        for dw in range(4):
            em.w("gpsimd", "PE", wo_free)
            em.op("gpsimd", lambda e, dw=dw: e.dma_start(
                out=bs(RC, NC * 512),
                in_=woT[:, dw * 512:(dw + 1) * 512].rearrange("(c p) n -> p c n", p=128)),
                inc="LW", n=16)
            lwv = em.cnt["LW"]
            em.w("tensor", "LW", max(lwv, lw_wuv))
            for lc in range(4):
                for j in range(NC):
                    mm(ps[0][:, :], bs(RB + j * NL + lc * 128, 128),
                       bs(RC + j * 512, 512), j == 0, j == NC - 1,
                       inc="PE" if j == NC - 1 else None)
                if lc == 3:
                    wo_free = em.cnt["PE"]
                em.w("scalar", "PE", em.cnt["PE"])
                em.op("scalar", lambda e, lc=lc: e.copy(
                    out=bs(RD + lc * 512, 512), in_=ps[0][:, :]), inc="SC")
                em.w("tensor", "SC", em.cnt["SC"])
            em.w("sync", "SC", em.cnt["SC"])
            em.op("sync", lambda e, dw=dw: e.dma_start(
                out=veffd[:, :, dw * 512:(dw + 1) * 512].rearrange("c p n -> p c n"),
                in_=bs(RD, 4 * 512)), inc="ST", n=16)
            em.w("scalar", "ST", em.cnt["ST"])
        em.barrier()

        def bcast(rowap, dst_off, bank=6, bf=True):
            em.w("tensor", "DV", em.cnt["DV"])
            em.w("tensor", "SC", em.cnt["SC"])
            mm(ps[bank][:, :], pf[0:1, of_onesf:of_onesf + 128], rowap, True, True, inc="PE")
            em.w("vector", "PE", em.cnt["PE"])
            if bf:
                em.op("vector", lambda e: e.tensor_copy(bs(dst_off, 512), ps[bank][:, :]), inc="DV")
            else:
                em.op("vector", lambda e: e.tensor_copy(fs(dst_off, 512), ps[bank][:, :]), inc="DV")
            em.w("tensor", "DV", em.cnt["DV"])
            return em.cnt["DV"]

        def sumsq_row(get_chunk, nchunks, scale, eps, out_row):
            for c in range(nchunks):
                src = get_chunk(c)
                em.w("scalar", "PE", em.cnt["PE"])
                em.op("scalar", lambda e, s=src: e.activation(
                    out=bs(o_sq, 512), in_=s, func=AF.Square), inc="SC")
                em.w("tensor", "SC", em.cnt["SC"])
                mm(ps[5][:, :], bs(o_ones, 128), bs(o_sq, 512), c == 0, c == nchunks - 1,
                   inc="PE")
            em.w("scalar", "PE", em.cnt["PE"])
            em.op("scalar", lambda e, sc=scale, ep=eps: e.activation(
                out=out_row, in_=ps[5][0:1, :], func=AF.Ln, bias=np.float32(ep),
                scale=np.float32(sc)), inc="SC")
            em.op("scalar", lambda e: e.activation(
                out=out_row, in_=out_row, func=AF.Exp, scale=np.float32(-0.5)), inc="SC")
            return em.cnt["SC"]

        lw_wdkv = cast_load(wdkvT, RD, NL)
        for w in range(NW):
            em.op("gpsimd", lambda e, w=w: e.dma_start(
                out=bs(RB, NC * 512),
                in_=xfm[:, w * 512:(w + 1) * 512].rearrange("(c p) n -> p c n", p=128)),
                inc="LW", n=16)
            lx0 = em.cnt["LW"]
            em.w("tensor", "LW", lx0)
            for c in range(NC):
                mm(ps[6][:, :], bs(o_ones, 128), bs(RB + c * 512, 512), c == 0, c == NC - 1,
                   inc="PE" if c == NC - 1 else None)
                em.w("scalar", "LW", lx0)
                em.w("scalar", "PE", em.cnt["PE"])
                em.op("scalar", lambda e, c=c: e.activation(
                    out=bs(o_sq, 512), in_=bs(RB + c * 512, 512), func=AF.Square), inc="SC")
                em.w("tensor", "SC", em.cnt["SC"])
                mm(ps[5][:, :], bs(o_ones, 128), bs(o_sq, 512), c == 0, c == NC - 1,
                   inc="PE")
            pe_st = em.cnt["PE"]
            em.w("vector", "PE", pe_st)
            em.op("vector", lambda e: e.tensor_scalar(
                out=row(r_t1, 512), in0=ps[6][0:1, :], scalar1=1.0 / D, scalar2=None,
                op0=ALU.mult), inc="DV")
            em.op("vector", lambda e: e.tensor_scalar(
                out=row(of_bc, 512), in0=ps[5][0:1, :], scalar1=1.0 / D, scalar2=None,
                op0=ALU.mult), inc="DV")
            em.op("vector", lambda e: e.tensor_tensor(
                out=row(of_bc + 512, 512), in0=row(r_t1, 512), in1=row(r_t1, 512),
                op=ALU.mult), inc="DV")
            em.op("vector", lambda e: e.tensor_tensor(
                out=row(of_bc, 512), in0=row(of_bc, 512), in1=row(of_bc + 512, 512),
                op=ALU.subtract), inc="DV")
            dv_var = em.cnt["DV"]
            em.w("scalar", "DV", dv_var)
            em.op("scalar", lambda e: e.activation(
                out=row(r_t2, 512), in_=row(of_bc, 512), func=AF.Ln,
                bias=np.float32(1e-5)), inc="SC")
            em.op("scalar", lambda e: e.activation(
                out=row(r_t2, 512), in_=row(r_t2, 512), func=AF.Exp,
                scale=np.float32(-0.5)), inc="SC")
            sc_a = em.cnt["SC"]
            em.w("vector", "SC", sc_a)
            em.op("vector", lambda e: e.tensor_tensor(
                out=row(r_t1, 512), in0=row(r_t1, 512), in1=row(r_t2, 512),
                op=ALU.mult), inc="DV")
            bcast(row(r_t2, 512), of_bc, bf=False)
            bcast(row(r_t1, 512), of_bc + 512, bf=False)
            em.w("vector", "DV", em.cnt["DV"])
            for c in range(NC):
                em.op("vector", lambda e, c=c, w=w: e.tensor_tensor(
                    out=fs(of_tmp, 512), in0=bs(RB + c * 512, 512), in1=fs(of_bc, 512),
                    op=ALU.mult), inc="DV")
                em.op("vector", lambda e, c=c, w=w: e.tensor_tensor(
                    out=bs(RF + c * T + w * 512, 512), in0=fs(of_tmp, 512),
                    in1=fs(of_bc + 512, 512), op=ALU.subtract), inc="DV")
            em.w("gpsimd", "DV", em.cnt["DV"])
        em.barrier()

        em.w("tensor", "LW", lw_wdkv)
        for lc in range(4):
            for w in range(NW):
                pk = ps[w % 4]
                for c in range(NC):
                    mm(pk[:, :], bs(RD + c * NL + lc * 128, 128),
                       bs(RF + c * T + w * 512, 512), c == 0, c == NC - 1,
                       inc="PE" if c == NC - 1 else None)
                em.w("scalar", "PE", em.cnt["PE"])
                em.op("scalar", lambda e, lc=lc, w=w, pk=pk: e.copy(
                    out=bs(RC + lc * T + w * 512, 512), in_=pk[:, :]), inc="SC")
                em.w("tensor", "SC", em.cnt["SC"])
        em.barrier()

        for m in range(NC):
            par = m % 2
            lwq = stream_cols(wqT[:, m * 128:(m + 1) * 128], 128, par)
            em.w("tensor", "LW", lwq)
            for w in range(NW):
                pk = ps[w % 4]
                for c in range(NC):
                    mm(pk[:, :], bs(o_wb[par] + c * 128, 128),
                       bs(RF + c * T + w * 512, 512), c == 0, c == NC - 1,
                       inc="PE" if c == NC - 1 else None)
                if w == NW - 1:
                    wb_free[par] = em.cnt["PE"]
                em.w("scalar", "PE", em.cnt["PE"])
                em.op("scalar", lambda e, w=w, pk=pk: e.copy(
                    out=bs(RD + w * 512, 512), in_=pk[:, :]), inc="SC")
            em.w("sync", "SC", em.cnt["SC"])
            em.op("sync", lambda e, m=m: e.dma_start(out=qd[m], in_=bs(RD, T)),
                  inc="ST", n=16)
            em.w("scalar", "ST", em.cnt["ST"])
        em.barrier()

        o_ckvt = RF
        for lc in range(4):
            for kc in range(NC):
                em.w("tensor", "DV", em.cnt["DV"])
                em.w("tensor", "SC", em.cnt["SC"])
                em.op("tensor", lambda e, lc=lc, kc=kc: e.transpose(
                    pT[:, 0:128], bs(RC + lc * T + kc * 128, 128), bs(o_identb, 128)),
                    inc="PE")
                em.w("vector", "PE", em.cnt["PE"])
                em.op("vector", lambda e, lc=lc, kc=kc: e.tensor_copy(
                    bs(o_ckvt + kc * NL + lc * 128, 128), pT[:, 0:128]), inc="DV")
        o_keff = RB
        o_veff = RF + 8192
        em.op("gpsimd", lambda e: e.dma_start(
            out=bs(o_keff, NH * NL), in_=keffd.rearrange("c p n -> p c n")), inc="LW", n=16)
        em.op("gpsimd", lambda e: e.dma_start(
            out=bs(o_veff, 4 * D), in_=veffd.rearrange("c p n -> p c n")), inc="LW", n=16)
        lw_kv = em.cnt["LW"]
        em.barrier()

        o_qhh = o_qh2
        probs_free = [0, 0]
        qh_free = [0, 0]
        for hh in range(NH):
            par = hh % 2
            em.w("sync", "PE", qh_free[par])
            em.op("sync", lambda e, hh=hh, par=par: e.dma_start(
                out=bs(o_qhh[par], T), in_=qd[hh]), inc="LX", n=16)
            lxq = em.cnt["LX"]
            for w in range(NW):
                nkc = 4 * (w + 1)
                em.w("tensor", "LX", lxq)
                em.w("tensor", "LW", lw_kv)
                for lc in range(4):
                    em.w("tensor", "SC", em.cnt["SC"])
                    mm(ps[4][:, :], bs(o_keff + hh * NL + lc * 128, 128),
                       bs(o_qhh[par] + w * 512, 512), True, True, inc="PE")
                    em.w("scalar", "PE", em.cnt["PE"])
                    em.op("scalar", lambda e, lc=lc: e.copy(
                        out=bs(o_qlat + lc * 512, 512), in_=ps[4][:, :]), inc="SC")
                sc_qlat = em.cnt["SC"]
                em.w("tensor", "SC", sc_qlat)
                em.op("sync", lambda e, hh=hh, w=w: e.dma_start(
                    out=fs(of_x0s[0], 512),
                    in_=xfm[hh * 128:(hh + 1) * 128, w * 512:(w + 1) * 512]), inc="LX", n=16)
                lx_x0 = em.cnt["LX"]
                for kc in range(nkc):
                    p2 = kc % 2
                    pS = ps[4 + p2]
                    for lc in range(4):
                        mm(pS[:, :], bs(RC + lc * T + kc * 128, 128),
                           bs(o_qlat + lc * 512, 512), lc == 0, lc == 3,
                           inc="PE" if lc == 3 else None)
                    pe_s = em.cnt["PE"]
                    em.w("scalar", "PE", max(pe_s, probs_free[p2]))
                    em.op("scalar", lambda e, p2=p2, pS=pS: e.activation(
                        out=bs(o_probs[p2], 512), in_=pS[:, :], func=AF.Exp,
                        bias=np.float32(-C_SHIFT), scale=np.float32(ISQ)), inc="SC")
                    sc_exp = em.cnt["SC"]
                    dep = ("SC", sc_exp)
                    delta = kc * 128 - w * 512
                    if delta >= 0:
                        em.w("vector", "SC", sc_exp)
                        em.op("vector", lambda e, p2=p2, d=delta: e.tensor_tensor(
                            out=bs(o_probs[p2], 512), in0=bs(o_probs[p2], 512),
                            in1=bs(o_cm + 384 - d, 512), op=ALU.mult), inc="DV")
                        dep = ("DV", em.cnt["DV"])
                    em.w("tensor", dep[0], dep[1])
                    for lc in range(4):
                        mm(ps[lc][:, :], bs(o_ckvt + kc * NL + lc * 128, 128),
                           bs(o_probs[p2], 512), kc == 0, kc == nkc - 1, inc=None)
                    mm(ps[6][:, :], bs(o_ones, 128), bs(o_probs[p2], 512),
                       kc == 0, kc == nkc - 1, inc="PE")
                    probs_free[p2] = em.cnt["PE"]
                pe_kd = em.cnt["PE"]
                em.w("scalar", "PE", pe_kd)
                em.op("scalar", lambda e: e.activation(
                    out=row(r_t1, 512), in_=ps[6][0:1, :], func=AF.Ln), inc="SC")
                em.op("scalar", lambda e: e.activation(
                    out=row(r_t1, 512), in_=row(r_t1, 512), func=AF.Exp,
                    scale=np.float32(-1.0)), inc="SC")
                sc_rec = em.cnt["SC"]
                em.w("tensor", "SC", sc_rec)
                mm(ps[6][:, :], pf[0:1, of_onesf:of_onesf + 128], row(r_t1, 512),
                   True, True, inc="PE")
                em.w("vector", "PE", em.cnt["PE"])
                em.op("vector", lambda e: e.tensor_copy(bs(o_rb, 512), ps[6][:, :]), inc="DV")
                dv_rb = em.cnt["DV"]
                for lc in range(4):
                    em.w("scalar", "PE", pe_kd)
                    em.op("scalar", lambda e, lc=lc: e.copy(
                        out=bs(o_ylat + lc * 512, 512), in_=ps[lc][:, :]), inc="SC")
                sc_ylat = em.cnt["SC"]
                em.w("tensor", "SC", sc_ylat)
                em.w("tensor", "DV", dv_rb)
                for lc in range(4):
                    mm(ps[4][:, :], bs(o_veff + lc * D + hh * 128, 128),
                       bs(o_ylat + lc * 512, 512), lc == 0, lc == 3,
                       inc="PE" if lc == 3 else None)
                pe_y = em.cnt["PE"]
                em.w("vector", "PE", pe_y)
                em.w("vector", "DV", dv_rb)
                em.w("vector", "LX", lx_x0)
                em.op("vector", lambda e: e.tensor_tensor(
                    out=fs(of_tmp, 512), in0=ps[4][:, :], in1=bs(o_rb, 512),
                    op=ALU.mult), inc="DV")
                em.op("vector", lambda e, w=w: e.tensor_tensor(
                    out=bs(o_kpre, 512), in0=fs(of_tmp, 512), in1=fs(of_x0s[0], 512),
                    op=ALU.add), inc="DV")
                x1dv = em.cnt["DV"]
                em.w("tensor", "DV", x1dv)
                em.w("sync", "DV", x1dv)
                em.op("sync", lambda e, hh=hh, w=w: e.dma_start(
                    out=x1d[hh, :, w * 512:(w + 1) * 512], in_=bs(o_kpre, 512)),
                    inc="ST", n=16)
                em.w("vector", "ST", em.cnt["ST"])
            qh_free[par] = em.cnt["PE"]
        em.barrier()

        o_h = RF
        for w in range(NW):
            em.op("sync", lambda e, w=w: e.dma_start(
                out=bs(RB, NC * 512),
                in_=x1d[:, :, w * 512:(w + 1) * 512].rearrange("c p n -> p c n")),
                inc="LX", n=16)
            lx1 = em.cnt["LX"]
            em.w("tensor", "LX", lx1)
            for c in range(NC):
                mm(ps[6][:, :], bs(o_ones, 128), bs(RB + c * 512, 512), c == 0, c == NC - 1,
                   inc="PE" if c == NC - 1 else None)
                em.w("scalar", "LX", lx1)
                em.w("scalar", "PE", em.cnt["PE"])
                em.op("scalar", lambda e, c=c: e.activation(
                    out=bs(o_sq, 512), in_=bs(RB + c * 512, 512), func=AF.Square), inc="SC")
                em.w("tensor", "SC", em.cnt["SC"])
                mm(ps[5][:, :], bs(o_ones, 128), bs(o_sq, 512), c == 0, c == NC - 1,
                   inc="PE")
            pe_st = em.cnt["PE"]
            em.w("vector", "PE", pe_st)
            em.op("vector", lambda e, w=w: e.tensor_scalar(
                out=row(r_m2 + w * 512, 512), in0=ps[6][0:1, :], scalar1=1.0 / D,
                scalar2=None, op0=ALU.mult), inc="DV")
            em.op("vector", lambda e: e.tensor_scalar(
                out=row(of_bc, 512), in0=ps[5][0:1, :], scalar1=1.0 / D, scalar2=None,
                op0=ALU.mult), inc="DV")
            em.op("vector", lambda e, w=w: e.tensor_tensor(
                out=row(of_bc + 512, 512), in0=row(r_m2 + w * 512, 512),
                in1=row(r_m2 + w * 512, 512), op=ALU.mult), inc="DV")
            em.op("vector", lambda e: e.tensor_tensor(
                out=row(of_bc, 512), in0=row(of_bc, 512), in1=row(of_bc + 512, 512),
                op=ALU.subtract), inc="DV")
            dv_var = em.cnt["DV"]
            em.w("scalar", "DV", dv_var)
            em.op("scalar", lambda e, w=w: e.activation(
                out=row(r_a2 + w * 512, 512), in_=row(of_bc, 512), func=AF.Ln,
                bias=np.float32(1e-5)), inc="SC")
            em.op("scalar", lambda e, w=w: e.activation(
                out=row(r_a2 + w * 512, 512), in_=row(r_a2 + w * 512, 512), func=AF.Exp,
                scale=np.float32(-0.5)), inc="SC")
            sc_a = em.cnt["SC"]
            em.w("vector", "SC", sc_a)
            em.op("vector", lambda e, w=w: e.tensor_tensor(
                out=row(r_t1, 512), in0=row(r_a2 + w * 512, 512),
                in1=row(r_a2 + w * 512, 512), op=ALU.mult), inc="DV")
            em.op("vector", lambda e: e.tensor_tensor(
                out=row(r_t1, 512), in0=row(of_bc, 512), in1=row(r_t1, 512),
                op=ALU.mult), inc="DV")
            dv_t = em.cnt["DV"]
            em.w("scalar", "DV", dv_t)
            em.op("scalar", lambda e: e.activation(
                out=row(r_t1, 512), in_=row(r_t1, 512), func=AF.Ln,
                bias=np.float32(1e-6)), inc="SC")
            em.op("scalar", lambda e: e.activation(
                out=row(r_t1, 512), in_=row(r_t1, 512), func=AF.Exp,
                scale=np.float32(-0.5)), inc="SC")
            em.w("vector", "SC", em.cnt["SC"])
            em.op("vector", lambda e, w=w: e.tensor_tensor(
                out=row(r_t1, 512), in0=row(r_a2 + w * 512, 512), in1=row(r_t1, 512),
                op=ALU.mult), inc="DV")
            em.op("vector", lambda e, w=w: e.tensor_tensor(
                out=row(r_t2, 512), in0=row(r_m2 + w * 512, 512), in1=row(r_t1, 512),
                op=ALU.mult), inc="DV")
            bcast(row(r_t1, 512), of_bc, bf=False)
            bcast(row(r_t2, 512), of_bc + 512, bf=False)
            em.w("vector", "DV", em.cnt["DV"])
            for c in range(NC):
                em.op("vector", lambda e, c=c, w=w: e.scalar_tensor_tensor(
                    out=fs(of_tmp, 512), in0=bs(RB + c * 512, 512),
                    scalar=fs(of_anw + c, 1), in1=fs(of_bc, 512),
                    op0=ALU.mult, op1=ALU.mult), inc="DV")
                em.op("vector", lambda e, c=c, w=w: e.scalar_tensor_tensor(
                    out=bs(o_h + c * T + w * 512, 512), in0=fs(of_bc + 512, 512),
                    scalar=fs(of_nanw + c, 1), in1=fs(of_tmp, 512),
                    op0=ALU.mult, op1=ALU.add), inc="DV")
            em.w("sync", "DV", em.cnt["DV"])
        em.barrier()

        o_kfm = RB
        o_vtm = RC
        lw_kwt = cast_load(kwT, RD, NL)

        def qk_norm_rope(dst_ap, w, wvec_of):
            sc0 = sumsq_row(lambda c: bs(o_kpre, 512), 1, 1.0 / HS, 1e-6, row(r_t1, 512))
            em.w("tensor", "SC", sc0)
            mm(ps[6][:, :], pf[0:1, of_onesf:of_onesf + 128], row(r_t1, 512), True, True,
               inc="PE")
            em.w("vector", "PE", em.cnt["PE"])
            em.op("vector", lambda e: e.tensor_copy(bs(o_gb, 512), ps[6][:, :]), inc="DV")
            em.op("vector", lambda e, wv=wvec_of: e.scalar_tensor_tensor(
                out=bs(o_kpre, 512), in0=bs(o_kpre, 512), scalar=fs(wv, 1),
                in1=bs(o_gb, 512), op0=ALU.mult, op1=ALU.mult), inc="DV")
            dv_n = em.cnt["DV"]
            em.w("sync", "DV", dv_n)
            em.op("sync", lambda e: e.dma_start(
                out=pb[0:64, o_swp:o_swp + 512], in_=pb[64:128, o_kpre:o_kpre + 512]),
                inc="LX", n=16)
            em.op("sync", lambda e: e.dma_start(
                out=pb[64:128, o_swp:o_swp + 512], in_=pb[0:64, o_kpre:o_kpre + 512]),
                inc="LX", n=16)
            lx_s = em.cnt["LX"]
            em.w("vector", "LX", lx_s)
            em.op("vector", lambda e, w=w: e.tensor_tensor(
                out=fs(of_tmp, 512), in0=bs(o_kpre, 512), in1=bs(o_ct + w * 512, 512),
                op=ALU.mult), inc="DV")
            em.op("vector", lambda e, w=w: e.tensor_tensor(
                out=fs(of_bc, 512), in0=bs(o_swp, 512), in1=bs(o_st + w * 512, 512),
                op=ALU.mult), inc="DV")
            em.op("vector", lambda e, d=dst_ap: e.tensor_tensor(
                out=d, in0=fs(of_tmp, 512), in1=fs(of_bc, 512), op=ALU.add), inc="DV")
            dv_d = em.cnt["DV"]
            em.w("sync", "DV", dv_d)
            em.w("scalar", "DV", dv_d)
            return dv_d

        em.w("tensor", "LW", lw_kwt)
        for kvh in range(HKV):
            for w in range(NW):
                for c in range(NC):
                    mm(ps[4][:, :], bs(RD + c * NL + kvh * 128, 128),
                       bs(o_h + c * T + w * 512, 512), c == 0, c == NC - 1,
                       inc="PE" if c == NC - 1 else None)
                em.w("scalar", "PE", em.cnt["PE"])
                em.w("scalar", "DV", em.cnt["DV"])
                em.op("scalar", lambda e: e.copy(out=bs(o_kpre, 512), in_=ps[4][:, :]), inc="SC")
                em.w("tensor", "SC", em.cnt["SC"])
                qk_norm_rope(bs(o_kfm + kvh * T + w * 512, 512), w, of_knw)
        em.w("gpsimd", "PE", em.cnt["PE"])
        lw_vwt = cast_load(vwT, RD, NL)
        em.w("tensor", "LW", lw_vwt)
        for kvh in range(HKV):
            for w in range(NW):
                for c in range(NC):
                    mm(ps[4][:, :], bs(RD + c * NL + kvh * 128, 128),
                       bs(o_h + c * T + w * 512, 512), c == 0, c == NC - 1,
                       inc="PE" if c == NC - 1 else None)
                em.w("scalar", "PE", em.cnt["PE"])
                em.w("scalar", "DV", em.cnt["DV"])
                em.op("scalar", lambda e: e.copy(out=bs(o_kpre, 512), in_=ps[4][:, :]), inc="SC")
                sc_v = em.cnt["SC"]
                em.w("tensor", "SC", sc_v)
                for t4 in range(4):
                    kc = w * 4 + t4
                    em.w("tensor", "DV", em.cnt["DV"])
                    em.w("tensor", "SC", em.cnt["SC"])
                    em.op("tensor", lambda e, t4=t4: e.transpose(
                        pT[:, 0:128], pb[:, o_kpre + t4 * 128:o_kpre + (t4 + 1) * 128],
                        bs(o_identb, 128)), inc="PE")
                    em.w("vector", "PE", em.cnt["PE"])
                    em.op("vector", lambda e, kvh=kvh, kc=kc: e.tensor_copy(
                        bs(o_vtm + kvh * T + kc * 128, 128), pT[:, 0:128]), inc="DV")
        for hh in range(NH):
            par = hh % 2
            lwq = stream_cols(qwT[:, hh * 128:(hh + 1) * 128], 128, par)
            em.w("tensor", "LW", lwq)
            for w in range(NW):
                for c in range(NC):
                    mm(ps[4][:, :], bs(o_wb[par] + c * 128, 128),
                       bs(o_h + c * T + w * 512, 512), c == 0, c == NC - 1,
                       inc="PE" if c == NC - 1 else None)
                if w == NW - 1:
                    wb_free[par] = em.cnt["PE"]
                em.w("scalar", "PE", em.cnt["PE"])
                em.w("scalar", "DV", em.cnt["DV"])
                em.w("scalar", "ST", em.cnt["ST"])
                em.op("scalar", lambda e: e.copy(out=bs(o_kpre, 512), in_=ps[4][:, :]), inc="SC")
                em.w("tensor", "SC", em.cnt["SC"])
                qk_norm_rope(bs(SH + w * 512, 512), w, of_qnw)
            dv_q = em.cnt["DV"]
            em.w("sync", "DV", dv_q)
            em.op("sync", lambda e, hh=hh: e.dma_start(out=qd[hh], in_=bs(SH, T)),
                  inc="ST", n=16)
            em.w("vector", "ST", em.cnt["ST"])
        em.barrier()

        probs_free = [0, 0]
        qh_free = [0, 0]
        for hh in range(NH):
            kvh = hh // 4
            par = hh % 2
            em.w("sync", "PE", qh_free[par])
            em.op("sync", lambda e, hh=hh, par=par: e.dma_start(
                out=bs(o_qh2[par], T), in_=qd[hh]), inc="LX", n=16)
            lxq = em.cnt["LX"]
            for w in range(NW):
                nkc = 4 * (w + 1)
                em.w("tensor", "LX", lxq)
                for kc in range(nkc):
                    p2 = kc % 2
                    pS = ps[4 + p2]
                    mm(pS[:, :], bs(o_kfm + kvh * T + kc * 128, 128),
                       bs(o_qh2[par] + w * 512, 512), True, True, inc="PE")
                    pe_s = em.cnt["PE"]
                    em.w("scalar", "PE", max(pe_s, probs_free[p2]))
                    em.op("scalar", lambda e, p2=p2, pS=pS: e.activation(
                        out=bs(o_probs[p2], 512), in_=pS[:, :], func=AF.Exp,
                        bias=np.float32(-C_SHIFT), scale=np.float32(ISQ)), inc="SC")
                    sc_exp = em.cnt["SC"]
                    dep = ("SC", sc_exp)
                    delta = kc * 128 - w * 512
                    if delta >= 0:
                        em.w("vector", "SC", sc_exp)
                        em.op("vector", lambda e, p2=p2, d=delta: e.tensor_tensor(
                            out=bs(o_probs[p2], 512), in0=bs(o_probs[p2], 512),
                            in1=bs(o_cm + 384 - d, 512), op=ALU.mult), inc="DV")
                        dep = ("DV", em.cnt["DV"])
                    em.w("tensor", dep[0], dep[1])
                    mm(ps[0][:, :], bs(o_vtm + kvh * T + kc * 128, 128),
                       bs(o_probs[p2], 512), kc == 0, kc == nkc - 1, inc=None)
                    mm(ps[6][:, :], bs(o_ones, 128), bs(o_probs[p2], 512),
                       kc == 0, kc == nkc - 1, inc="PE")
                    probs_free[p2] = em.cnt["PE"]
                pe_kd = em.cnt["PE"]
                em.w("scalar", "PE", pe_kd)
                em.op("scalar", lambda e: e.activation(
                    out=row(r_t1, 512), in_=ps[6][0:1, :], func=AF.Ln), inc="SC")
                em.op("scalar", lambda e: e.activation(
                    out=row(r_t1, 512), in_=row(r_t1, 512), func=AF.Exp,
                    scale=np.float32(-1.0)), inc="SC")
                em.w("tensor", "SC", em.cnt["SC"])
                mm(ps[6][:, :], pf[0:1, of_onesf:of_onesf + 128], row(r_t1, 512),
                   True, True, inc="PE")
                em.w("vector", "PE", em.cnt["PE"])
                em.op("vector", lambda e: e.tensor_copy(bs(o_rb, 512), ps[6][:, :]), inc="DV")
                em.op("vector", lambda e: e.tensor_tensor(
                    out=bs(o_kpre, 512), in0=ps[0][:, :], in1=bs(o_rb, 512),
                    op=ALU.mult), inc="DV")
                y2dv = em.cnt["DV"]
                em.w("tensor", "DV", y2dv)
                em.w("sync", "DV", y2dv)
                em.op("sync", lambda e, hh=hh, w=w: e.dma_start(
                    out=y2d[hh, :, w * 512:(w + 1) * 512], in_=bs(o_kpre, 512)),
                    inc="ST", n=16)
                em.w("vector", "ST", em.cnt["ST"])
            qh_free[par] = em.cnt["PE"]
        em.barrier()

        for w in range(NW):
            em.op("sync", lambda e, w=w: e.dma_start(
                out=bs(RD, NC * 512),
                in_=y2d[:, :, w * 512:(w + 1) * 512].rearrange("c p n -> p c n")),
                inc="LX", n=16)
            lxy = em.cnt["LX"]
            em.op("sync", lambda e, w=w: e.dma_start(
                out=bs(RB, NC * 512),
                in_=x1d[:, :, w * 512:(w + 1) * 512].rearrange("c p n -> p c n")),
                inc="LX", n=16)
            lxx = em.cnt["LX"]
            em.op("vector", lambda e, w=w: e.tensor_tensor(
                out=row(r_t1, 512), in0=row(r_m2 + w * 512, 512),
                in1=row(r_a2 + w * 512, 512), op=ALU.mult), inc="DV")
            bcast(row(r_a2 + w * 512, 512), of_bc, bf=False)
            bcast(row(r_t1, 512), of_bc + 512, bf=False)
            dv_bc = em.cnt["DV"]
            for dc in range(NC):
                par = dc % 2
                lwo = stream_cols(owT[:, dc * 128:(dc + 1) * 128], 128, par)
                em.w("tensor", "LW", lwo)
                em.w("tensor", "LX", lxy)
                for yc in range(NC):
                    mm(ps[4][:, :], bs(o_wb[par] + yc * 128, 128),
                       bs(RD + yc * 512, 512), yc == 0, yc == NC - 1,
                       inc="PE" if yc == NC - 1 else None)
                wb_free[par] = em.cnt["PE"]
                pe_p = em.cnt["PE"]
                em.w("vector", "PE", pe_p)
                em.w("vector", "LX", lxx)
                em.w("vector", "DV", dv_bc)
                em.op("vector", lambda e, dc=dc: e.tensor_tensor(
                    out=fs(of_tmp, 512), in0=bs(RB + dc * 512, 512), in1=fs(of_bc, 512),
                    op=ALU.mult), inc="DV")
                em.op("vector", lambda e: e.tensor_tensor(
                    out=fs(of_tmp, 512), in0=fs(of_tmp, 512), in1=fs(of_bc + 512, 512),
                    op=ALU.subtract), inc="DV")
                em.w("vector", "ST", em.cnt["ST"])
                em.op("vector", lambda e, dc=dc, w=w: e.tensor_tensor(
                    out=bs(o_kpre, 512), in0=fs(of_tmp, 512),
                    in1=ps[4][:, :], op=ALU.add), inc="DV")
                g2dv = em.cnt["DV"]
                em.w("tensor", "DV", g2dv)
                em.w("sync", "DV", g2dv)
                em.op("sync", lambda e, dc=dc, w=w: e.dma_start(
                    out=g2d[dc, :, w * 512:(w + 1) * 512], in_=bs(o_kpre, 512)),
                    inc="ST", n=16)
            em.w("sync", "DV", em.cnt["DV"])
        em.barrier()

        o_h2 = RF
        for w in range(NW):
            em.op("sync", lambda e, w=w: e.dma_start(
                out=bs(RB, NC * 512),
                in_=g2d[:, :, w * 512:(w + 1) * 512].rearrange("c p n -> p c n")),
                inc="LX", n=16)
            lxg = em.cnt["LX"]
            em.w("scalar", "LX", lxg)
            for c in range(NC):
                em.w("scalar", "PE", em.cnt["PE"])
                em.op("scalar", lambda e, c=c: e.activation(
                    out=bs(o_sq, 512), in_=bs(RB + c * 512, 512),
                    func=AF.Square), inc="SC")
                em.w("tensor", "SC", em.cnt["SC"])
                mm(ps[5][:, :], bs(o_ones, 128), bs(o_sq, 512), c == 0, c == NC - 1,
                   inc="PE")
            em.w("scalar", "PE", em.cnt["PE"])
            em.op("scalar", lambda e: e.activation(
                out=row(r_t1, 512), in_=ps[5][0:1, :], func=AF.Ln,
                bias=np.float32(1e-6), scale=np.float32(1.0 / D)), inc="SC")
            em.op("scalar", lambda e: e.activation(
                out=row(r_t1, 512), in_=row(r_t1, 512), func=AF.Exp,
                scale=np.float32(-0.5)), inc="SC")
            em.w("tensor", "SC", em.cnt["SC"])
            bcast(row(r_t1, 512), o_gb, bf=True)
            em.w("vector", "DV", em.cnt["DV"])
            for c in range(NC):
                em.op("vector", lambda e, c=c, w=w: e.scalar_tensor_tensor(
                    out=bs(o_h2 + c * T + w * 512, 512),
                    in0=bs(RB + c * 512, 512), scalar=fs(of_fnw + c, 1),
                    in1=bs(o_gb, 512), op0=ALU.mult, op1=ALU.mult), inc="DV")
            em.w("sync", "DV", em.cnt["DV"])
        em.barrier()

        o_comb = RB
        for tt in range(NC):
            for c in range(NC):
                mm(ps[4][:, 0:8], bs(o_h2 + c * T + tt * 128, 128),
                   bs(o_gw + c * EXP, EXP), c == 0, c == NC - 1,
                   inc="PE" if c == NC - 1 else None)
            pe_g = em.cnt["PE"]
            em.w("vector", "PE", pe_g)
            g_lg = fs(of_bc, 8)
            g_e = fs(of_bc + 8, 8)
            g_m1 = fs(of_bc + 16, 1)
            g_z = fs(of_bc + 17, 1)
            g_t = fs(of_bc + 18, 8)
            g_m2 = fs(of_bc + 26, 1)
            g_den = fs(of_bc + 27, 1)
            g_keep = fs(of_bc + 28, 8)
            em.op("vector", lambda e: e.tensor_copy(g_lg, ps[4][:, 0:8]), inc="DV")
            em.op("vector", lambda e: e.reduce_max(g_m1, g_lg, axis=AX.X), inc="DV")
            em.op("vector", lambda e: e.tensor_scalar(
                out=g_e, in0=g_lg, scalar1=g_m1, scalar2=None, op0=ALU.subtract), inc="DV")
            em.w("scalar", "DV", em.cnt["DV"])
            em.op("scalar", lambda e: e.activation(out=g_e, in_=g_e, func=AF.Exp), inc="SC")
            em.w("vector", "SC", em.cnt["SC"])
            em.op("vector", lambda e: e.tensor_reduce(
                g_z, g_e, axis=AX.X, op=ALU.add), inc="DV")
            em.op("vector", lambda e: e.reciprocal(g_z, g_z), inc="DV")
            em.op("vector", lambda e: e.tensor_scalar(
                out=g_e, in0=g_e, scalar1=g_z, scalar2=None, op0=ALU.mult), inc="DV")
            em.op("vector", lambda e: e.reduce_max(g_m1, g_e, axis=AX.X), inc="DV")
            em.op("vector", lambda e: e.tensor_scalar(
                out=g_t, in0=g_e, scalar1=g_m1, scalar2=None, op0=ALU.is_lt), inc="DV")
            em.op("vector", lambda e: e.tensor_tensor(
                out=g_t, in0=g_e, in1=g_t, op=ALU.mult), inc="DV")
            em.op("vector", lambda e: e.reduce_max(g_m2, g_t, axis=AX.X), inc="DV")
            em.op("vector", lambda e: e.tensor_tensor(
                out=g_den, in0=g_m1, in1=g_m2, op=ALU.add), inc="DV")
            em.op("vector", lambda e: e.tensor_scalar(
                out=g_den, in0=g_den, scalar1=1e-9, scalar2=None, op0=ALU.add), inc="DV")
            em.op("vector", lambda e: e.reciprocal(g_den, g_den), inc="DV")
            em.op("vector", lambda e: e.tensor_scalar(
                out=g_keep, in0=g_e, scalar1=g_m2, scalar2=None, op0=ALU.is_ge), inc="DV")
            em.op("vector", lambda e: e.tensor_tensor(
                out=g_keep, in0=g_e, in1=g_keep, op=ALU.mult), inc="DV")
            em.op("vector", lambda e: e.tensor_scalar(
                out=g_keep, in0=g_keep, scalar1=g_den, scalar2=None, op0=ALU.mult), inc="DV")
            dv_c = em.cnt["DV"]
            em.w("tensor", "DV", dv_c)
            for grp in range(2):
                em.w("tensor", "DV", em.cnt["DV"])
                em.w("tensor", "SC", em.cnt["SC"])
                for si in range(4):
                    ei = grp * 4 + si
                    em.op("tensor", lambda e, ei=ei, si=si: e.transpose(
                        ps[5][0:1, si * 128:(si + 1) * 128], g_keep[:, ei:ei + 1],
                        fs(of_identf, 128)), inc="PE")
                em.w("vector", "PE", em.cnt["PE"])
                for si in range(4):
                    ei = grp * 4 + si
                    em.op("vector", lambda e, ei=ei, si=si, tt=tt: e.tensor_copy(
                        pb[0:1, o_comb + ei * T + tt * 128:o_comb + ei * T + (tt + 1) * 128],
                        ps[5][0:1, si * 128:(si + 1) * 128]), inc="DV")
        em.barrier()

        o_mwb = [o_wb[0], o_wb[1], o_qh2[0], o_qh2[1]]
        mwb_free = [0, 0, 0, 0]
        o_crow = of_bc
        for ei in range(EXP):
            for w in range(NW):
                em.w("tensor", "DV", em.cnt["DV"])
                em.w("tensor", "SC", em.cnt["SC"])
                mm(ps[6][:, :], pb[0:1, o_ones:o_ones + 128],
                   pb[0:1, o_comb + ei * T + w * 512:o_comb + ei * T + (w + 1) * 512],
                   True, True, inc="PE")
                em.w("vector", "PE", em.cnt["PE"])
                em.op("vector", lambda e, w=w: e.tensor_copy(
                    bs(o_cb + w * 512, 512), ps[6][:, :]), inc="DV")
                em.w("tensor", "DV", em.cnt["DV"])
            dv_cb = em.cnt["DV"]
            for ic in range(11):
                pg = ic % 2
                pu = 2 + ic % 2
                lwg = None
                em.w("gpsimd", "PE", mwb_free[pg])
                em.op("gpsimd", lambda e, o=o_mwb[pg], ei=ei, ic=ic: e.dma_start(
                    out=bs(o, NC * 128),
                    in_=wgT[ei, :, ic * 128:(ic + 1) * 128].rearrange("(c p) n -> p c n", p=128)),
                    inc="LW", n=16)
                lwg = em.cnt["LW"]
                em.w("gpsimd", "PE", mwb_free[pu])
                em.op("gpsimd", lambda e, o=o_mwb[pu], ei=ei, ic=ic: e.dma_start(
                    out=bs(o, NC * 128),
                    in_=wuT[ei, :, ic * 128:(ic + 1) * 128].rearrange("(c p) n -> p c n", p=128)),
                    inc="LW", n=16)
                lwu = em.cnt["LW"]
                em.w("tensor", "LW", lwu)
                em.w("tensor", "SC", em.cnt["SC"])
                for w in range(NW):
                    for c in range(NC):
                        mm(ps[w][:, :], bs(o_mwb[pg] + c * 128, 128),
                           bs(o_h2 + c * T + w * 512, 512), c == 0, c == NC - 1,
                           inc="PE" if (c == NC - 1 and w == NW - 1) else None)
                mwb_free[pg] = em.cnt["PE"]
                for w in range(NW):
                    pk = ps[4 + (w % 2)]
                    for c in range(NC):
                        mm(pk[:, :], bs(o_mwb[pu] + c * 128, 128),
                           bs(o_h2 + c * T + w * 512, 512), c == 0, c == NC - 1,
                           inc="PE" if c == NC - 1 else None)
                    pe_eu = em.cnt["PE"]
                    if w == NW - 1:
                        mwb_free[pu] = pe_eu
                    em.w("scalar", "PE", pe_eu)
                    em.op("scalar", lambda e, w=w: e.activation(
                        out=bs(o_sg, 512), in_=ps[w][:, :], func=AF.Silu), inc="SC")
                    sc_sg = em.cnt["SC"]
                    em.w("vector", "SC", sc_sg)
                    em.w("vector", "DV", dv_cb)
                    em.w("vector", "ST", em.cnt["ST"])
                    em.op("vector", lambda e, pk=pk: e.tensor_tensor(
                        out=fs(of_tmp, 512), in0=pk[:, :], in1=bs(o_sg, 512),
                        op=ALU.mult), inc="DV")
                    em.op("vector", lambda e, w=w: e.tensor_tensor(
                        out=bs(o_actc + w * 512, 512), in0=fs(of_tmp, 512),
                        in1=bs(o_cb + w * 512, 512), op=ALU.mult), inc="DV")
                    em.w("tensor", "DV", em.cnt["DV"])
                dv_act = em.cnt["DV"]
                em.w("sync", "DV", dv_act)
                em.op("sync", lambda e, ei=ei, ic=ic: e.dma_start(
                    out=actd[ei, ic], in_=bs(o_actc, 4 * 512)), inc="ST", n=16)
        em.barrier()

        o_rw = RF
        o_g2w = RF + 8192
        o_acte = CB
        o_wdb = o_wb
        wdb_free = [0, 0]
        for qtr in range(4):
            em.w("gpsimd", "DV", em.cnt["DV"])
            em.w("gpsimd", "PE", em.cnt["PE"])
            em.op("gpsimd", lambda e, qtr=qtr: e.dma_start(
                out=bs(o_emq, NC * 256),
                in_=emat_i[:, qtr * 256:(qtr + 1) * 256].rearrange(
                    "(c p) n -> p c n", p=128)), inc="LW", n=16)
            lw_e = em.cnt["LW"]
            for w in range(NW):
                em.w("sync", "DV", em.cnt["DV"])
                em.op("sync", lambda e, w=w: e.dma_start(
                    out=bs(o_rw, NC * 512),
                    in_=x1d[:, :, w * 512:(w + 1) * 512].rearrange("c p n -> p c n")),
                    inc="LX", n=16)
                em.op("sync", lambda e, w=w: e.dma_start(
                    out=bs(o_g2w, NC * 512),
                    in_=g2d[:, :, w * 512:(w + 1) * 512].rearrange("c p n -> p c n")),
                    inc="LX", n=16)
                lxr = em.cnt["LX"]
                em.w("vector", "LX", lxr)
                for c in range(NC):
                    em.op("vector", lambda e, c=c: e.tensor_tensor(
                        out=bs(o_rw + c * 512, 512), in0=bs(o_rw + c * 512, 512),
                        in1=bs(o_g2w + c * 512, 512), op=ALU.add), inc="DV")
                dv_r = em.cnt["DV"]
                em.w("tensor", "DV", dv_r)
                em.w("tensor", "LW", lw_e)
                for dq in range(2):
                    pk = ps[dq % 2]
                    for c in range(NC):
                        mm(pk[:, :], bs(o_emq + c * 256 + dq * 128, 128),
                           bs(o_rw + c * 512, 512), c == 0, c == NC - 1,
                           inc="PE" if c == NC - 1 else None)
                    em.w("vector", "PE", em.cnt["PE"])
                    em.op("vector", lambda e, dq=dq, w=w, pk=pk: e.tensor_copy(
                        oacc[:, dq * T + w * 512:dq * T + (w + 1) * 512], pk[:, :]), inc="DV")
                    em.w("tensor", "DV", em.cnt["DV"])
            for ei in range(EXP):
                em.w("sync", "DV", em.cnt["DV"])
                em.op("sync", lambda e, ei=ei: e.dma_start(
                    out=bs(o_acte, 11 * T), in_=actd[ei].rearrange("c p n -> p c n")),
                    inc="LX", n=16)
                lxa = em.cnt["LX"]
                em.w("tensor", "LX", lxa)
                for dq in range(2):
                    dcp = qtr * 2 + dq
                    par = dq % 2
                    em.w("gpsimd", "PE", wdb_free[par])
                    em.op("gpsimd", lambda e, o=o_wdb[par], ei=ei, dcp=dcp: e.dma_start(
                        out=bs(o, 11 * 128),
                        in_=wdTh[ei, :, dcp * 128:(dcp + 1) * 128].rearrange(
                            "(c p) n -> p c n", p=128)), inc="LW", n=16)
                    lwd = em.cnt["LW"]
                    em.w("tensor", "LW", lwd)
                    for w in range(NW):
                        pk = ps[4 + (w % 2)]
                        for ic in range(11):
                            mm(pk[:, :], bs(o_wdb[par] + ic * 128, 128),
                               bs(o_acte + ic * T + w * 512, 512), ic == 0, ic == 10,
                               inc="PE" if ic == 10 else None)
                        if w == NW - 1:
                            wdb_free[par] = em.cnt["PE"]
                        em.w("vector", "PE", em.cnt["PE"])
                        em.op("vector", lambda e, dq=dq, w=w, pk=pk: e.tensor_tensor(
                            out=oacc[:, dq * T + w * 512:dq * T + (w + 1) * 512],
                            in0=oacc[:, dq * T + w * 512:dq * T + (w + 1) * 512],
                            in1=pk[:, :], op=ALU.add), inc="DV")
                        em.w("tensor", "DV", em.cnt["DV"])
            em.w("sync", "DV", em.cnt["DV"])
            for dq in range(2):
                dcp = qtr * 2 + dq
                em.op("sync", lambda e, dq=dq, dcp=dcp: e.dma_start(
                    out=outp[dcp * 128:(dcp + 1) * 128, :],
                    in_=oacc[:, dq * T:(dq + 1) * T]), inc="ST", n=16)
            st_h = em.cnt["ST"]
            em.w("vector", "ST", st_h)
        em.w("sync", "ST", em.cnt["ST"])
        em.barrier()

        @blk.tensor
        def _(e):
            for f in em.ops["tensor"]:
                f(e)

        @blk.vector
        def _(e):
            for f in em.ops["vector"]:
                f(e)

        @blk.scalar
        def _(e):
            for f in em.ops["scalar"]:
                f(e)

        @blk.sync
        def _(e):
            for f in em.ops["sync"]:
                f(e)

        @blk.gpsimd
        def _(e):
            for f in em.ops["gpsimd"]:
                f(e)

    return nc


def _rope_tables():
    freqs = 1.0 / (ROPE_THETA ** (np.arange(0, HS, 2, dtype=np.float64) / HS))
    t = np.arange(T, dtype=np.float64)
    f = np.outer(freqs, t)
    ct = np.concatenate([np.cos(f), np.cos(f)], axis=0)
    st = np.concatenate([-np.sin(f), np.sin(f)], axis=0)
    return ct.astype(BF), st.astype(BF)


_NC_CACHE = {}


def kernel(L0_attn_x, L0_attn_Wq_w, L0_attn_Wdkv_w, L0_attn_Wuk_w, L0_attn_Wuv_w,
           L0_attn_Wo_w, L0_ffn_attn_norm_w, L0_ffn_q_w, L0_ffn_k_w, L0_ffn_v_w,
           L0_ffn_o_w, L0_ffn_q_norm_w, L0_ffn_k_norm_w, L0_ffn_ffn_norm_w,
           L0_ffn_gate_w, L0_ffn_wg, L0_ffn_wu, L0_ffn_wd):
    x = np.asarray(L0_attn_x, f32)
    B = x.shape[0]
    ac = np.ascontiguousarray

    wqn = ac(np.asarray(L0_attn_Wq_w, f32))
    wqT = ac(wqn.T)
    wuk = ac(np.asarray(L0_attn_Wuk_w, f32))
    wuv = ac(np.asarray(L0_attn_Wuv_w, f32))
    woT = ac(np.asarray(L0_attn_Wo_w, f32).T)
    wdkvT = ac(np.asarray(L0_attn_Wdkv_w, f32).T)
    qwT = ac(np.asarray(L0_ffn_q_w, f32).T)
    kwT = ac(np.asarray(L0_ffn_k_w, f32).T)
    vwT = ac(np.asarray(L0_ffn_v_w, f32).T)
    owT = ac(np.asarray(L0_ffn_o_w, f32).T)
    gwT = ac(np.asarray(L0_ffn_gate_w, f32).T)
    wgT = ac(np.transpose(np.asarray(L0_ffn_wg, f32), (0, 2, 1)))
    wuT = ac(np.transpose(np.asarray(L0_ffn_wu, f32), (0, 2, 1)))
    wdT = np.transpose(np.asarray(L0_ffn_wd, f32), (0, 2, 1))

    anw = ac(np.asarray(L0_ffn_attn_norm_w, f32).reshape(NC, 128).T)
    fnw = ac(np.asarray(L0_ffn_ffn_norm_w, f32).reshape(NC, 128).T)
    qnw = ac(np.asarray(L0_ffn_q_norm_w, f32).reshape(128, 1))
    knw = ac(np.asarray(L0_ffn_k_norm_w, f32).reshape(128, 1))

    ct, st = _rope_tables()
    jj = np.arange(896)[None, :]
    ii = np.arange(128)[:, None]
    cm = (jj >= ii + 384).astype(BF)
    onesb = np.ones((128, 128), dtype=BF)
    onesf = np.ones((1, 128), dtype=f32)
    identb = np.eye(128, dtype=BF)
    identf = np.eye(128, dtype=f32)

    if "nc" not in _NC_CACHE:
        _NC_CACHE["nc"] = build_nc()
    nc = _NC_CACHE["nc"]

    halves = []
    for half in range(2):
        wdTh = ac(wdT[:, :, half * 1024:(half + 1) * 1024])
        em_np = np.zeros((D, 1024), dtype=BF)
        em_np[half * 1024:(half + 1) * 1024, :] = np.eye(1024, dtype=BF)
        halves.append((wdTh, em_np))

    in_maps = []
    for d in range(8):
        b = d // 2
        wdTh, em_np = halves[d % 2]
        in_maps.append({
            "xfm": ac(x[b].T), "wqn": wqn, "wuk": wuk, "wuv": wuv, "woT": woT,
            "wdkvT": wdkvT, "wqT": wqT, "qwT": qwT, "kwT": kwT, "vwT": vwT,
            "owT": owT, "gwT": gwT, "wgT": wgT, "wuT": wuT, "wdTh": wdTh,
            "emat": em_np, "anw": anw, "fnw": fnw, "qnw": qnw, "knw": knw,
            "ct": ct, "st": st, "cm": cm, "onesb": onesb, "onesf": onesf,
            "identb": identb, "identf": identf,
        })

    res = run_bass_kernel_spmd(nc, in_maps, core_ids=list(range(8)))
    out = np.empty((B, T, D), dtype=f32)
    for b in range(B):
        top = res.results[2 * b]["outp"]
        bot = res.results[2 * b + 1]["outp"]
        out[b] = np.concatenate([top, bot], axis=0).T
    return out


# version 03
# speedup vs baseline: 1.1488x; 1.1488x over previous
import numpy as np
import ml_dtypes
import contextlib
import concourse.bass as bass
import concourse.mybir as mybir
from concourse.bass_utils import run_bass_kernel_spmd

F32 = mybir.dt.float32
BF16 = mybir.dt.bfloat16
AF = mybir.ActivationFunctionType
ALU = mybir.AluOpType
AX = mybir.AxisListType
BF = ml_dtypes.bfloat16

T = 2048
D = 2048
NL = 512
NH = 16
HS = 128
HKV = 4
EXP = 8
FI = 1408
NW = 4
NC = 16
C_SHIFT = 24.0
ROPE_THETA = 1e6
ISQ = 1.0 / np.sqrt(128.0)

f32 = np.float32


class Emitter:
    ENGS = ("tensor", "vector", "scalar", "sync", "gpsimd")

    def __init__(self, nc):
        self.nc = nc
        self.ops = {e: [] for e in self.ENGS}
        self.cnt = {}
        self.sems = {}
        self.waited = {e: {} for e in self.ENGS}

    def sem(self, name, stack):
        h = stack.enter_context(self.nc.semaphore(name))
        self.sems[name] = h
        self.cnt[name] = 0

    def w(self, eng, sname, val):
        if val is None or val <= 0:
            return
        if self.waited[eng].get(sname, -1) >= val:
            return
        self.waited[eng][sname] = val
        h = self.sems[sname]
        self.ops[eng].append(lambda e, h=h, v=val: e.wait_ge(h, v))

    ENG_SEM = {"vector": "DV", "scalar": "SC"}

    def op(self, eng, fn, inc=None, n=1, guard=True):
        s = self.ENG_SEM.get(eng)
        if s is not None:
            self.w(eng, s, self.cnt[s])
        if n > 1:
            if guard:
                for s2 in ("PE", "SC", "DV"):
                    self.w(eng, s2, self.cnt[s2])
            if inc is not None:
                self.w(eng, inc, self.cnt[inc])
        if inc is None:
            self.ops[eng].append(lambda e, fn=fn: fn(e))
            return None
        h = self.sems[inc]
        self.cnt[inc] += n
        self.ops[eng].append(lambda e, fn=fn, h=h, n=n: fn(e).then_inc(h, n))
        return self.cnt[inc]

    def barrier(self):
        snap = dict(self.cnt)
        for e in self.ENGS:
            for s, v in snap.items():
                self.w(e, s, v)


def build_nc():
    nc = bass.Bass("TRN2", target_bir_lowering=False)
    dt = nc.dram_tensor

    xfm = dt("xfm", [D, T], F32, kind="ExternalInput").ap()
    wqn = dt("wqn", [D, D], F32, kind="ExternalInput").ap()
    wuk = dt("wuk", [D, NL], F32, kind="ExternalInput").ap()
    wuv = dt("wuv", [D, NL], F32, kind="ExternalInput").ap()
    woT = dt("woT", [D, D], F32, kind="ExternalInput").ap()
    wdkvT = dt("wdkvT", [D, NL], F32, kind="ExternalInput").ap()
    wqT = dt("wqT", [D, D], F32, kind="ExternalInput").ap()
    qwT = dt("qwT", [D, D], F32, kind="ExternalInput").ap()
    kwT = dt("kwT", [D, NL], F32, kind="ExternalInput").ap()
    vwT = dt("vwT", [D, NL], F32, kind="ExternalInput").ap()
    owT = dt("owT", [D, D], F32, kind="ExternalInput").ap()
    gwT = dt("gwT", [D, EXP], F32, kind="ExternalInput").ap()
    wgT = dt("wgT", [EXP, D, FI], F32, kind="ExternalInput").ap()
    wuT = dt("wuT", [EXP, D, FI], F32, kind="ExternalInput").ap()
    wdTh = dt("wdTh", [EXP, FI, 1024], F32, kind="ExternalInput").ap()
    emat_i = dt("emat", [D, 1024], BF16, kind="ExternalInput").ap()
    anw_i = dt("anw", [128, NC], F32, kind="ExternalInput").ap()
    fnw_i = dt("fnw", [128, NC], F32, kind="ExternalInput").ap()
    qnw_i = dt("qnw", [128, 1], F32, kind="ExternalInput").ap()
    knw_i = dt("knw", [128, 1], F32, kind="ExternalInput").ap()
    ct_i = dt("ct", [128, T], BF16, kind="ExternalInput").ap()
    st_i = dt("st", [128, T], BF16, kind="ExternalInput").ap()
    cm_i = dt("cm", [128, 896], BF16, kind="ExternalInput").ap()
    ones_i = dt("onesb", [128, 128], BF16, kind="ExternalInput").ap()
    onesf_i = dt("onesf", [1, 128], F32, kind="ExternalInput").ap()
    identb_i = dt("identb", [128, 128], BF16, kind="ExternalInput").ap()
    identf_i = dt("identf", [128, 128], F32, kind="ExternalInput").ap()

    outp = dt("outp", [1024, T], F32, kind="ExternalOutput").ap()

    keffd = dt("keffd", [NH, 128, NL], BF16)
    veffd = dt("veffd", [4, 128, D], BF16)
    qd = dt("qd", [NH, 128, T], BF16)
    x1d = dt("x1d", [NC, 128, T], BF16)
    y2d = dt("y2d", [NC, 128, T], BF16)
    g2d = dt("g2d", [NC, 128, T], BF16)
    actd = dt("actd", [EXP, 11, 128, T], BF16)

    with contextlib.ExitStack() as stack:
        En = stack.enter_context
        PB_N = 79104
        PF_N = 7488
        pb = En(nc.sbuf_tensor("pb", [128, PB_N], BF16))
        pf = En(nc.sbuf_tensor("pf", [128, PF_N], F32))
        oacc = En(nc.sbuf_tensor("oacc", [128, 2 * T], F32))
        ps = [En(nc.psum_tensor(f"ps{i}", [128, 512], F32)) for i in range(7)]
        pT = En(nc.psum_tensor("pT", [128, 1024], BF16))

        em = Emitter(nc)
        for s in ("LW", "LX", "PE", "DV", "SC", "ST"):
            em.sem(s, stack)
        blk = En(nc.Block())

        o_ct = 0
        o_st = 2048
        o_cm = 4096
        o_ones = 4992
        o_identb = 5120
        o_gw = 5248
        CB = 5376
        RB = CB
        RC = CB + 8192
        RD = CB + 16384
        RF = CB + 24576
        SMALL = RF + 32768
        o_probs = [SMALL, SMALL + 512]
        o_sq = SMALL + 1024
        SH = SMALL + 1536
        o_qlat = SH
        o_ylat = SH + 2048
        o_actc = SH
        o_cb = SH + 2048
        o_emq = SH
        o_rb = SMALL + 5632
        o_swp = SMALL + 6144
        o_gb = SMALL + 6656
        o_wb = [SMALL + 7168, SMALL + 9216]
        o_qh2 = [SMALL + 11264, SMALL + 13312]
        o_kpre = SMALL + 15360
        o_sg = SMALL + 15872
        PB_NEED = SMALL + 16384
        assert PB_NEED == PB_N, (PB_NEED, PB_N)

        of_anw = 0
        of_nanw = 16
        of_fnw = 32
        of_qnw = 48
        of_knw = 49
        of_onesf = 64
        of_identf = 192
        of_x0s = [320, 320]
        of_bc = 832
        of_tmp = 1856
        r_m2 = 2368
        r_a2 = 4416
        r_t1 = 6464
        r_t2 = 6976
        assert 7488 <= PF_N

        def bs(off, w):
            return pb[:, off:off + w]

        def fs(off, w):
            return pf[:, off:off + w]

        def row(off, w):
            return pf[0:1, off:off + w]

        for dst, src in [
            (bs(o_ct, T), ct_i), (bs(o_st, T), st_i), (bs(o_cm, 896), cm_i),
            (bs(o_ones, 128), ones_i), (bs(o_identb, 128), identb_i),
            (bs(o_gw, NC * EXP), gwT.rearrange("(c p) e -> p c e", p=128)),
            (fs(of_anw, NC), anw_i), (fs(of_fnw, NC), fnw_i),
            (fs(of_qnw, 1), qnw_i), (fs(of_knw, 1), knw_i),
            (pf[0:1, of_onesf:of_onesf + 128], onesf_i),
            (fs(of_identf, 128), identf_i),
        ]:
            em.op("gpsimd", lambda e, d=dst, s=src: e.dma_start(out=d, in_=s), inc="LW", n=16)
        lw_consts = em.cnt["LW"]
        em.w("vector", "LW", lw_consts)
        em.op("vector", lambda e: e.tensor_scalar(
            out=fs(of_nanw, NC), in0=fs(of_anw, NC), scalar1=-1.0, scalar2=None,
            op0=ALU.mult), inc="DV")
        em.w("tensor", "LW", lw_consts)

        def cast_load(dram_ap, off, width, eng="gpsimd"):
            em.w(eng, "PE", em.cnt["PE"])
            em.op(eng, lambda e, a=dram_ap, o=off, w=width: e.dma_start(
                out=bs(o, NC * w), in_=a.rearrange("(c p) n -> p c n", p=128)),
                inc="LW", n=16)
            return em.cnt["LW"]

        def mm(out, lhsT, rhs, start, stop, inc=None):
            if start:
                em.w("tensor", "SC", em.cnt["SC"])
                em.w("tensor", "DV", em.cnt["DV"])
            return em.op("tensor", lambda e, o=out, l=lhsT, r=rhs, s1=start, s2=stop:
                         e.matmul(o, l, r, start=s1, stop=s2), inc=inc)

        wb_free = [0, 0]

        def stream_cols(dram_ap, ncols, par):
            em.w("gpsimd", "PE", wb_free[par])
            em.op("gpsimd", lambda e, a=dram_ap, o=o_wb[par], w=ncols: e.dma_start(
                out=bs(o, NC * w), in_=a.rearrange("(c p) n -> p c n", p=128)),
                inc="LW", n=16, guard=False)
            return em.cnt["LW"]

        lw_wuk = cast_load(wuk, RB, NL)
        for m in range(NC):
            par = m % 2
            lwv = stream_cols(wqn[:, m * 128:(m + 1) * 128], 128, par)
            em.w("tensor", "LW", max(lwv, lw_wuk))
            for j in range(NC):
                mm(ps[0][:, :], bs(o_wb[par] + j * 128, 128), bs(RB + j * NL, NL),
                   j == 0, j == NC - 1, inc="PE" if j == NC - 1 else None)
            wb_free[par] = em.cnt["PE"]
            em.w("scalar", "PE", em.cnt["PE"])
            em.op("scalar", lambda e: e.copy(out=bs(RD, NL), in_=ps[0][:, :]), inc="SC")
            sc_e = em.cnt["SC"]
            em.w("tensor", "SC", sc_e)
            em.w("sync", "SC", sc_e)
            em.op("sync", lambda e, m=m: e.dma_start(
                out=keffd[m], in_=bs(RD, NL)), inc="ST", n=16)
            em.w("scalar", "ST", em.cnt["ST"])
        lw_wuv = cast_load(wuv, RB, NL)
        wo_free = 0
        for dw in range(4):
            em.w("gpsimd", "PE", wo_free)
            em.op("gpsimd", lambda e, dw=dw: e.dma_start(
                out=bs(RC, NC * 512),
                in_=woT[:, dw * 512:(dw + 1) * 512].rearrange("(c p) n -> p c n", p=128)),
                inc="LW", n=16)
            lwv = em.cnt["LW"]
            em.w("tensor", "LW", max(lwv, lw_wuv))
            for lc in range(4):
                for j in range(NC):
                    mm(ps[0][:, :], bs(RB + j * NL + lc * 128, 128),
                       bs(RC + j * 512, 512), j == 0, j == NC - 1,
                       inc="PE" if j == NC - 1 else None)
                if lc == 3:
                    wo_free = em.cnt["PE"]
                em.w("scalar", "PE", em.cnt["PE"])
                em.op("scalar", lambda e, lc=lc: e.copy(
                    out=bs(RD + lc * 512, 512), in_=ps[0][:, :]), inc="SC")
                em.w("tensor", "SC", em.cnt["SC"])
            em.w("sync", "SC", em.cnt["SC"])
            em.op("sync", lambda e, dw=dw: e.dma_start(
                out=veffd[:, :, dw * 512:(dw + 1) * 512].rearrange("c p n -> p c n"),
                in_=bs(RD, 4 * 512)), inc="ST", n=16)
            em.w("scalar", "ST", em.cnt["ST"])
        em.barrier()

        def bcast(rowap, dst_off, bank=6, bf=True):
            em.w("tensor", "DV", em.cnt["DV"])
            em.w("tensor", "SC", em.cnt["SC"])
            mm(ps[bank][:, :], pf[0:1, of_onesf:of_onesf + 128], rowap, True, True, inc="PE")
            em.w("vector", "PE", em.cnt["PE"])
            if bf:
                em.op("vector", lambda e: e.tensor_copy(bs(dst_off, 512), ps[bank][:, :]), inc="DV")
            else:
                em.op("vector", lambda e: e.tensor_copy(fs(dst_off, 512), ps[bank][:, :]), inc="DV")
            em.w("tensor", "DV", em.cnt["DV"])
            return em.cnt["DV"]

        def sumsq_row(get_chunk, nchunks, scale, eps, out_row):
            for c in range(nchunks):
                src = get_chunk(c)
                em.w("scalar", "PE", em.cnt["PE"])
                em.op("scalar", lambda e, s=src: e.activation(
                    out=bs(o_sq, 512), in_=s, func=AF.Square), inc="SC")
                em.w("tensor", "SC", em.cnt["SC"])
                mm(ps[5][:, :], bs(o_ones, 128), bs(o_sq, 512), c == 0, c == nchunks - 1,
                   inc="PE")
            em.w("scalar", "PE", em.cnt["PE"])
            em.op("scalar", lambda e, sc=scale, ep=eps: e.activation(
                out=out_row, in_=ps[5][0:1, :], func=AF.Ln, bias=np.float32(ep),
                scale=np.float32(sc)), inc="SC")
            em.op("scalar", lambda e: e.activation(
                out=out_row, in_=out_row, func=AF.Exp, scale=np.float32(-0.5)), inc="SC")
            return em.cnt["SC"]

        lw_wdkv = cast_load(wdkvT, RD, NL)
        for w in range(NW):
            em.op("gpsimd", lambda e, w=w: e.dma_start(
                out=bs(RB, NC * 512),
                in_=xfm[:, w * 512:(w + 1) * 512].rearrange("(c p) n -> p c n", p=128)),
                inc="LW", n=16)
            lx0 = em.cnt["LW"]
            em.w("tensor", "LW", lx0)
            for c in range(NC):
                mm(ps[6][:, :], bs(o_ones, 128), bs(RB + c * 512, 512), c == 0, c == NC - 1,
                   inc="PE" if c == NC - 1 else None)
                em.w("scalar", "LW", lx0)
                em.w("scalar", "PE", em.cnt["PE"])
                em.op("scalar", lambda e, c=c: e.activation(
                    out=bs(o_sq, 512), in_=bs(RB + c * 512, 512), func=AF.Square), inc="SC")
                em.w("tensor", "SC", em.cnt["SC"])
                mm(ps[5][:, :], bs(o_ones, 128), bs(o_sq, 512), c == 0, c == NC - 1,
                   inc="PE")
            pe_st = em.cnt["PE"]
            em.w("vector", "PE", pe_st)
            em.op("vector", lambda e: e.tensor_scalar(
                out=row(r_t1, 512), in0=ps[6][0:1, :], scalar1=1.0 / D, scalar2=None,
                op0=ALU.mult), inc="DV")
            em.op("vector", lambda e: e.tensor_scalar(
                out=row(of_bc, 512), in0=ps[5][0:1, :], scalar1=1.0 / D, scalar2=None,
                op0=ALU.mult), inc="DV")
            em.op("vector", lambda e: e.tensor_tensor(
                out=row(of_bc + 512, 512), in0=row(r_t1, 512), in1=row(r_t1, 512),
                op=ALU.mult), inc="DV")
            em.op("vector", lambda e: e.tensor_tensor(
                out=row(of_bc, 512), in0=row(of_bc, 512), in1=row(of_bc + 512, 512),
                op=ALU.subtract), inc="DV")
            dv_var = em.cnt["DV"]
            em.w("scalar", "DV", dv_var)
            em.op("scalar", lambda e: e.activation(
                out=row(r_t2, 512), in_=row(of_bc, 512), func=AF.Ln,
                bias=np.float32(1e-5)), inc="SC")
            em.op("scalar", lambda e: e.activation(
                out=row(r_t2, 512), in_=row(r_t2, 512), func=AF.Exp,
                scale=np.float32(-0.5)), inc="SC")
            sc_a = em.cnt["SC"]
            em.w("vector", "SC", sc_a)
            em.op("vector", lambda e: e.tensor_tensor(
                out=row(r_t1, 512), in0=row(r_t1, 512), in1=row(r_t2, 512),
                op=ALU.mult), inc="DV")
            bcast(row(r_t2, 512), of_bc, bf=False)
            bcast(row(r_t1, 512), of_bc + 512, bf=False)
            em.w("vector", "DV", em.cnt["DV"])
            for c in range(NC):
                em.op("vector", lambda e, c=c, w=w: e.tensor_tensor(
                    out=fs(of_tmp, 512), in0=bs(RB + c * 512, 512), in1=fs(of_bc, 512),
                    op=ALU.mult), inc="DV")
                em.op("vector", lambda e, c=c, w=w: e.tensor_tensor(
                    out=bs(RF + c * T + w * 512, 512), in0=fs(of_tmp, 512),
                    in1=fs(of_bc + 512, 512), op=ALU.subtract), inc="DV")
            em.w("gpsimd", "DV", em.cnt["DV"])
        em.barrier()

        em.w("tensor", "LW", lw_wdkv)
        for lc in range(4):
            for w in range(NW):
                pk = ps[w % 4]
                for c in range(NC):
                    mm(pk[:, :], bs(RD + c * NL + lc * 128, 128),
                       bs(RF + c * T + w * 512, 512), c == 0, c == NC - 1,
                       inc="PE" if c == NC - 1 else None)
                em.w("scalar", "PE", em.cnt["PE"])
                em.op("scalar", lambda e, lc=lc, w=w, pk=pk: e.copy(
                    out=bs(RC + lc * T + w * 512, 512), in_=pk[:, :]), inc="SC")
                em.w("tensor", "SC", em.cnt["SC"])
        em.barrier()

        for m in range(NC):
            par = m % 2
            lwq = stream_cols(wqT[:, m * 128:(m + 1) * 128], 128, par)
            em.w("tensor", "LW", lwq)
            for w in range(NW):
                pk = ps[w % 4]
                for c in range(NC):
                    mm(pk[:, :], bs(o_wb[par] + c * 128, 128),
                       bs(RF + c * T + w * 512, 512), c == 0, c == NC - 1,
                       inc="PE" if c == NC - 1 else None)
                if w == NW - 1:
                    wb_free[par] = em.cnt["PE"]
                em.w("scalar", "PE", em.cnt["PE"])
                em.op("scalar", lambda e, w=w, pk=pk: e.copy(
                    out=bs(RD + w * 512, 512), in_=pk[:, :]), inc="SC")
            em.w("sync", "SC", em.cnt["SC"])
            em.op("sync", lambda e, m=m: e.dma_start(out=qd[m], in_=bs(RD, T)),
                  inc="ST", n=16)
            em.w("scalar", "ST", em.cnt["ST"])
        em.barrier()

        o_ckvt = RF
        for lc in range(4):
            for kc in range(NC):
                em.w("tensor", "DV", em.cnt["DV"])
                em.w("tensor", "SC", em.cnt["SC"])
                em.op("tensor", lambda e, lc=lc, kc=kc: e.transpose(
                    pT[:, 0:128], bs(RC + lc * T + kc * 128, 128), bs(o_identb, 128)),
                    inc="PE")
                em.w("vector", "PE", em.cnt["PE"])
                em.op("vector", lambda e, lc=lc, kc=kc: e.tensor_copy(
                    bs(o_ckvt + kc * NL + lc * 128, 128), pT[:, 0:128]), inc="DV")
        o_keff = RB
        o_veff = RF + 8192
        em.op("gpsimd", lambda e: e.dma_start(
            out=bs(o_keff, NH * NL), in_=keffd.rearrange("c p n -> p c n")), inc="LW", n=16)
        em.op("gpsimd", lambda e: e.dma_start(
            out=bs(o_veff, 4 * D), in_=veffd.rearrange("c p n -> p c n")), inc="LW", n=16)
        lw_kv = em.cnt["LW"]
        em.barrier()

        o_qhh = o_qh2
        probs_free = [0, 0]
        qh_free = [0, 0]
        for hh in range(NH):
            par = hh % 2
            em.w("sync", "PE", qh_free[par])
            em.op("sync", lambda e, hh=hh, par=par: e.dma_start(
                out=bs(o_qhh[par], T), in_=qd[hh]), inc="LX", n=16, guard=False)
            lxq = em.cnt["LX"]
            for w in range(NW):
                nkc = 4 * (w + 1)
                em.w("tensor", "LX", lxq)
                em.w("tensor", "LW", lw_kv)
                for lc in range(4):
                    em.w("tensor", "SC", em.cnt["SC"])
                    mm(ps[4][:, :], bs(o_keff + hh * NL + lc * 128, 128),
                       bs(o_qhh[par] + w * 512, 512), True, True, inc="PE")
                    em.w("scalar", "PE", em.cnt["PE"])
                    em.op("scalar", lambda e, lc=lc: e.copy(
                        out=bs(o_qlat + lc * 512, 512), in_=ps[4][:, :]), inc="SC")
                sc_qlat = em.cnt["SC"]
                em.w("tensor", "SC", sc_qlat)
                em.op("sync", lambda e, hh=hh, w=w: e.dma_start(
                    out=fs(of_x0s[0], 512),
                    in_=xfm[hh * 128:(hh + 1) * 128, w * 512:(w + 1) * 512]), inc="LX", n=16)
                lx_x0 = em.cnt["LX"]
                for kc in range(nkc):
                    p2 = kc % 2
                    pS = ps[4 + p2]
                    for lc in range(4):
                        mm(pS[:, :], bs(RC + lc * T + kc * 128, 128),
                           bs(o_qlat + lc * 512, 512), lc == 0, lc == 3,
                           inc="PE" if lc == 3 else None)
                    pe_s = em.cnt["PE"]
                    em.w("scalar", "PE", max(pe_s, probs_free[p2]))
                    em.op("scalar", lambda e, p2=p2, pS=pS: e.activation(
                        out=bs(o_probs[p2], 512), in_=pS[:, :], func=AF.Exp,
                        bias=np.float32(-C_SHIFT), scale=np.float32(ISQ)), inc="SC")
                    sc_exp = em.cnt["SC"]
                    dep = ("SC", sc_exp)
                    delta = kc * 128 - w * 512
                    if delta >= 0:
                        em.w("vector", "SC", sc_exp)
                        em.op("vector", lambda e, p2=p2, d=delta: e.tensor_tensor(
                            out=bs(o_probs[p2], 512), in0=bs(o_probs[p2], 512),
                            in1=bs(o_cm + 384 - d, 512), op=ALU.mult), inc="DV")
                        dep = ("DV", em.cnt["DV"])
                    em.w("tensor", dep[0], dep[1])
                    for lc in range(4):
                        mm(ps[lc][:, :], bs(o_ckvt + kc * NL + lc * 128, 128),
                           bs(o_probs[p2], 512), kc == 0, kc == nkc - 1, inc=None)
                    mm(ps[6][:, :], bs(o_ones, 128), bs(o_probs[p2], 512),
                       kc == 0, kc == nkc - 1, inc="PE")
                    probs_free[p2] = em.cnt["PE"]
                pe_kd = em.cnt["PE"]
                em.w("scalar", "PE", pe_kd)
                em.op("scalar", lambda e: e.activation(
                    out=row(r_t1, 512), in_=ps[6][0:1, :], func=AF.Ln), inc="SC")
                em.op("scalar", lambda e: e.activation(
                    out=row(r_t1, 512), in_=row(r_t1, 512), func=AF.Exp,
                    scale=np.float32(-1.0)), inc="SC")
                sc_rec = em.cnt["SC"]
                em.w("tensor", "SC", sc_rec)
                mm(ps[6][:, :], pf[0:1, of_onesf:of_onesf + 128], row(r_t1, 512),
                   True, True, inc="PE")
                em.w("vector", "PE", em.cnt["PE"])
                em.op("vector", lambda e: e.tensor_copy(bs(o_rb, 512), ps[6][:, :]), inc="DV")
                dv_rb = em.cnt["DV"]
                for lc in range(4):
                    em.w("scalar", "PE", pe_kd)
                    em.op("scalar", lambda e, lc=lc: e.copy(
                        out=bs(o_ylat + lc * 512, 512), in_=ps[lc][:, :]), inc="SC")
                sc_ylat = em.cnt["SC"]
                em.w("tensor", "SC", sc_ylat)
                em.w("tensor", "DV", dv_rb)
                for lc in range(4):
                    mm(ps[4][:, :], bs(o_veff + lc * D + hh * 128, 128),
                       bs(o_ylat + lc * 512, 512), lc == 0, lc == 3,
                       inc="PE" if lc == 3 else None)
                pe_y = em.cnt["PE"]
                em.w("vector", "PE", pe_y)
                em.w("vector", "DV", dv_rb)
                em.w("vector", "LX", lx_x0)
                em.op("vector", lambda e: e.tensor_tensor(
                    out=fs(of_tmp, 512), in0=ps[4][:, :], in1=bs(o_rb, 512),
                    op=ALU.mult), inc="DV")
                em.op("vector", lambda e, w=w: e.tensor_tensor(
                    out=bs(o_kpre, 512), in0=fs(of_tmp, 512), in1=fs(of_x0s[0], 512),
                    op=ALU.add), inc="DV")
                x1dv = em.cnt["DV"]
                em.w("tensor", "DV", x1dv)
                em.w("sync", "DV", x1dv)
                em.op("sync", lambda e, hh=hh, w=w: e.dma_start(
                    out=x1d[hh, :, w * 512:(w + 1) * 512], in_=bs(o_kpre, 512)),
                    inc="ST", n=16)
                em.w("vector", "ST", em.cnt["ST"])
            qh_free[par] = em.cnt["PE"]
        em.barrier()

        o_h = RF
        for w in range(NW):
            em.op("sync", lambda e, w=w: e.dma_start(
                out=bs(RB, NC * 512),
                in_=x1d[:, :, w * 512:(w + 1) * 512].rearrange("c p n -> p c n")),
                inc="LX", n=16)
            lx1 = em.cnt["LX"]
            em.w("tensor", "LX", lx1)
            for c in range(NC):
                mm(ps[6][:, :], bs(o_ones, 128), bs(RB + c * 512, 512), c == 0, c == NC - 1,
                   inc="PE" if c == NC - 1 else None)
                em.w("scalar", "LX", lx1)
                em.w("scalar", "PE", em.cnt["PE"])
                em.op("scalar", lambda e, c=c: e.activation(
                    out=bs(o_sq, 512), in_=bs(RB + c * 512, 512), func=AF.Square), inc="SC")
                em.w("tensor", "SC", em.cnt["SC"])
                mm(ps[5][:, :], bs(o_ones, 128), bs(o_sq, 512), c == 0, c == NC - 1,
                   inc="PE")
            pe_st = em.cnt["PE"]
            em.w("vector", "PE", pe_st)
            em.op("vector", lambda e, w=w: e.tensor_scalar(
                out=row(r_m2 + w * 512, 512), in0=ps[6][0:1, :], scalar1=1.0 / D,
                scalar2=None, op0=ALU.mult), inc="DV")
            em.op("vector", lambda e: e.tensor_scalar(
                out=row(of_bc, 512), in0=ps[5][0:1, :], scalar1=1.0 / D, scalar2=None,
                op0=ALU.mult), inc="DV")
            em.op("vector", lambda e, w=w: e.tensor_tensor(
                out=row(of_bc + 512, 512), in0=row(r_m2 + w * 512, 512),
                in1=row(r_m2 + w * 512, 512), op=ALU.mult), inc="DV")
            em.op("vector", lambda e: e.tensor_tensor(
                out=row(of_bc, 512), in0=row(of_bc, 512), in1=row(of_bc + 512, 512),
                op=ALU.subtract), inc="DV")
            dv_var = em.cnt["DV"]
            em.w("scalar", "DV", dv_var)
            em.op("scalar", lambda e, w=w: e.activation(
                out=row(r_a2 + w * 512, 512), in_=row(of_bc, 512), func=AF.Ln,
                bias=np.float32(1e-5)), inc="SC")
            em.op("scalar", lambda e, w=w: e.activation(
                out=row(r_a2 + w * 512, 512), in_=row(r_a2 + w * 512, 512), func=AF.Exp,
                scale=np.float32(-0.5)), inc="SC")
            sc_a = em.cnt["SC"]
            em.w("vector", "SC", sc_a)
            em.op("vector", lambda e, w=w: e.tensor_tensor(
                out=row(r_t1, 512), in0=row(r_a2 + w * 512, 512),
                in1=row(r_a2 + w * 512, 512), op=ALU.mult), inc="DV")
            em.op("vector", lambda e: e.tensor_tensor(
                out=row(r_t1, 512), in0=row(of_bc, 512), in1=row(r_t1, 512),
                op=ALU.mult), inc="DV")
            dv_t = em.cnt["DV"]
            em.w("scalar", "DV", dv_t)
            em.op("scalar", lambda e: e.activation(
                out=row(r_t1, 512), in_=row(r_t1, 512), func=AF.Ln,
                bias=np.float32(1e-6)), inc="SC")
            em.op("scalar", lambda e: e.activation(
                out=row(r_t1, 512), in_=row(r_t1, 512), func=AF.Exp,
                scale=np.float32(-0.5)), inc="SC")
            em.w("vector", "SC", em.cnt["SC"])
            em.op("vector", lambda e, w=w: e.tensor_tensor(
                out=row(r_t1, 512), in0=row(r_a2 + w * 512, 512), in1=row(r_t1, 512),
                op=ALU.mult), inc="DV")
            em.op("vector", lambda e, w=w: e.tensor_tensor(
                out=row(r_t2, 512), in0=row(r_m2 + w * 512, 512), in1=row(r_t1, 512),
                op=ALU.mult), inc="DV")
            bcast(row(r_t1, 512), of_bc, bf=False)
            bcast(row(r_t2, 512), of_bc + 512, bf=False)
            em.w("vector", "DV", em.cnt["DV"])
            for c in range(NC):
                em.op("vector", lambda e, c=c, w=w: e.scalar_tensor_tensor(
                    out=fs(of_tmp, 512), in0=bs(RB + c * 512, 512),
                    scalar=fs(of_anw + c, 1), in1=fs(of_bc, 512),
                    op0=ALU.mult, op1=ALU.mult), inc="DV")
                em.op("vector", lambda e, c=c, w=w: e.scalar_tensor_tensor(
                    out=bs(o_h + c * T + w * 512, 512), in0=fs(of_bc + 512, 512),
                    scalar=fs(of_nanw + c, 1), in1=fs(of_tmp, 512),
                    op0=ALU.mult, op1=ALU.add), inc="DV")
            em.w("sync", "DV", em.cnt["DV"])
        em.barrier()

        o_kfm = RB
        o_vtm = RC
        lw_kwt = cast_load(kwT, RD, NL)

        def qk_norm_rope(dst_ap, w, wvec_of):
            sc0 = sumsq_row(lambda c: bs(o_kpre, 512), 1, 1.0 / HS, 1e-6, row(r_t1, 512))
            em.w("tensor", "SC", sc0)
            mm(ps[6][:, :], pf[0:1, of_onesf:of_onesf + 128], row(r_t1, 512), True, True,
               inc="PE")
            em.w("vector", "PE", em.cnt["PE"])
            em.op("vector", lambda e: e.tensor_copy(bs(o_gb, 512), ps[6][:, :]), inc="DV")
            em.op("vector", lambda e, wv=wvec_of: e.scalar_tensor_tensor(
                out=bs(o_kpre, 512), in0=bs(o_kpre, 512), scalar=fs(wv, 1),
                in1=bs(o_gb, 512), op0=ALU.mult, op1=ALU.mult), inc="DV")
            dv_n = em.cnt["DV"]
            em.w("sync", "DV", dv_n)
            em.op("sync", lambda e: e.dma_start(
                out=pb[0:64, o_swp:o_swp + 512], in_=pb[64:128, o_kpre:o_kpre + 512]),
                inc="LX", n=16)
            em.op("sync", lambda e: e.dma_start(
                out=pb[64:128, o_swp:o_swp + 512], in_=pb[0:64, o_kpre:o_kpre + 512]),
                inc="LX", n=16)
            lx_s = em.cnt["LX"]
            em.w("vector", "LX", lx_s)
            em.op("vector", lambda e, w=w: e.tensor_tensor(
                out=fs(of_tmp, 512), in0=bs(o_kpre, 512), in1=bs(o_ct + w * 512, 512),
                op=ALU.mult), inc="DV")
            em.op("vector", lambda e, w=w: e.tensor_tensor(
                out=fs(of_bc, 512), in0=bs(o_swp, 512), in1=bs(o_st + w * 512, 512),
                op=ALU.mult), inc="DV")
            em.op("vector", lambda e, d=dst_ap: e.tensor_tensor(
                out=d, in0=fs(of_tmp, 512), in1=fs(of_bc, 512), op=ALU.add), inc="DV")
            dv_d = em.cnt["DV"]
            em.w("sync", "DV", dv_d)
            em.w("scalar", "DV", dv_d)
            return dv_d

        em.w("tensor", "LW", lw_kwt)
        for kvh in range(HKV):
            for w in range(NW):
                for c in range(NC):
                    mm(ps[4][:, :], bs(RD + c * NL + kvh * 128, 128),
                       bs(o_h + c * T + w * 512, 512), c == 0, c == NC - 1,
                       inc="PE" if c == NC - 1 else None)
                em.w("scalar", "PE", em.cnt["PE"])
                em.w("scalar", "DV", em.cnt["DV"])
                em.op("scalar", lambda e: e.copy(out=bs(o_kpre, 512), in_=ps[4][:, :]), inc="SC")
                em.w("tensor", "SC", em.cnt["SC"])
                qk_norm_rope(bs(o_kfm + kvh * T + w * 512, 512), w, of_knw)
        em.w("gpsimd", "PE", em.cnt["PE"])
        lw_vwt = cast_load(vwT, RD, NL)
        em.w("tensor", "LW", lw_vwt)
        for kvh in range(HKV):
            for w in range(NW):
                for c in range(NC):
                    mm(ps[4][:, :], bs(RD + c * NL + kvh * 128, 128),
                       bs(o_h + c * T + w * 512, 512), c == 0, c == NC - 1,
                       inc="PE" if c == NC - 1 else None)
                em.w("scalar", "PE", em.cnt["PE"])
                em.w("scalar", "DV", em.cnt["DV"])
                em.op("scalar", lambda e: e.copy(out=bs(o_kpre, 512), in_=ps[4][:, :]), inc="SC")
                sc_v = em.cnt["SC"]
                em.w("tensor", "SC", sc_v)
                for t4 in range(4):
                    kc = w * 4 + t4
                    em.w("tensor", "DV", em.cnt["DV"])
                    em.w("tensor", "SC", em.cnt["SC"])
                    em.op("tensor", lambda e, t4=t4: e.transpose(
                        pT[:, 0:128], pb[:, o_kpre + t4 * 128:o_kpre + (t4 + 1) * 128],
                        bs(o_identb, 128)), inc="PE")
                    em.w("vector", "PE", em.cnt["PE"])
                    em.op("vector", lambda e, kvh=kvh, kc=kc: e.tensor_copy(
                        bs(o_vtm + kvh * T + kc * 128, 128), pT[:, 0:128]), inc="DV")
        for hh in range(NH):
            par = hh % 2
            lwq = stream_cols(qwT[:, hh * 128:(hh + 1) * 128], 128, par)
            em.w("tensor", "LW", lwq)
            for w in range(NW):
                for c in range(NC):
                    mm(ps[4][:, :], bs(o_wb[par] + c * 128, 128),
                       bs(o_h + c * T + w * 512, 512), c == 0, c == NC - 1,
                       inc="PE" if c == NC - 1 else None)
                if w == NW - 1:
                    wb_free[par] = em.cnt["PE"]
                em.w("scalar", "PE", em.cnt["PE"])
                em.w("scalar", "DV", em.cnt["DV"])
                em.w("scalar", "ST", em.cnt["ST"])
                em.op("scalar", lambda e: e.copy(out=bs(o_kpre, 512), in_=ps[4][:, :]), inc="SC")
                em.w("tensor", "SC", em.cnt["SC"])
                qk_norm_rope(bs(SH + w * 512, 512), w, of_qnw)
            dv_q = em.cnt["DV"]
            em.w("sync", "DV", dv_q)
            em.op("sync", lambda e, hh=hh: e.dma_start(out=qd[hh], in_=bs(SH, T)),
                  inc="ST", n=16)
            em.w("vector", "ST", em.cnt["ST"])
        em.barrier()

        probs_free = [0, 0]
        qh_free = [0, 0]
        for hh in range(NH):
            kvh = hh // 4
            par = hh % 2
            em.w("sync", "PE", qh_free[par])
            em.op("sync", lambda e, hh=hh, par=par: e.dma_start(
                out=bs(o_qh2[par], T), in_=qd[hh]), inc="LX", n=16, guard=False)
            lxq = em.cnt["LX"]
            for w in range(NW):
                nkc = 4 * (w + 1)
                em.w("tensor", "LX", lxq)
                for kc in range(nkc):
                    p2 = kc % 2
                    pS = ps[4 + p2]
                    mm(pS[:, :], bs(o_kfm + kvh * T + kc * 128, 128),
                       bs(o_qh2[par] + w * 512, 512), True, True, inc="PE")
                    pe_s = em.cnt["PE"]
                    em.w("scalar", "PE", max(pe_s, probs_free[p2]))
                    em.op("scalar", lambda e, p2=p2, pS=pS: e.activation(
                        out=bs(o_probs[p2], 512), in_=pS[:, :], func=AF.Exp,
                        bias=np.float32(-C_SHIFT), scale=np.float32(ISQ)), inc="SC")
                    sc_exp = em.cnt["SC"]
                    dep = ("SC", sc_exp)
                    delta = kc * 128 - w * 512
                    if delta >= 0:
                        em.w("vector", "SC", sc_exp)
                        em.op("vector", lambda e, p2=p2, d=delta: e.tensor_tensor(
                            out=bs(o_probs[p2], 512), in0=bs(o_probs[p2], 512),
                            in1=bs(o_cm + 384 - d, 512), op=ALU.mult), inc="DV")
                        dep = ("DV", em.cnt["DV"])
                    em.w("tensor", dep[0], dep[1])
                    mm(ps[0][:, :], bs(o_vtm + kvh * T + kc * 128, 128),
                       bs(o_probs[p2], 512), kc == 0, kc == nkc - 1, inc=None)
                    mm(ps[6][:, :], bs(o_ones, 128), bs(o_probs[p2], 512),
                       kc == 0, kc == nkc - 1, inc="PE")
                    probs_free[p2] = em.cnt["PE"]
                pe_kd = em.cnt["PE"]
                em.w("scalar", "PE", pe_kd)
                em.op("scalar", lambda e: e.activation(
                    out=row(r_t1, 512), in_=ps[6][0:1, :], func=AF.Ln), inc="SC")
                em.op("scalar", lambda e: e.activation(
                    out=row(r_t1, 512), in_=row(r_t1, 512), func=AF.Exp,
                    scale=np.float32(-1.0)), inc="SC")
                em.w("tensor", "SC", em.cnt["SC"])
                mm(ps[6][:, :], pf[0:1, of_onesf:of_onesf + 128], row(r_t1, 512),
                   True, True, inc="PE")
                em.w("vector", "PE", em.cnt["PE"])
                em.op("vector", lambda e: e.tensor_copy(bs(o_rb, 512), ps[6][:, :]), inc="DV")
                em.op("vector", lambda e: e.tensor_tensor(
                    out=bs(o_kpre, 512), in0=ps[0][:, :], in1=bs(o_rb, 512),
                    op=ALU.mult), inc="DV")
                y2dv = em.cnt["DV"]
                em.w("tensor", "DV", y2dv)
                em.w("sync", "DV", y2dv)
                em.op("sync", lambda e, hh=hh, w=w: e.dma_start(
                    out=y2d[hh, :, w * 512:(w + 1) * 512], in_=bs(o_kpre, 512)),
                    inc="ST", n=16)
                em.w("vector", "ST", em.cnt["ST"])
            qh_free[par] = em.cnt["PE"]
        em.barrier()

        for w in range(NW):
            em.op("sync", lambda e, w=w: e.dma_start(
                out=bs(RD, NC * 512),
                in_=y2d[:, :, w * 512:(w + 1) * 512].rearrange("c p n -> p c n")),
                inc="LX", n=16)
            lxy = em.cnt["LX"]
            em.op("sync", lambda e, w=w: e.dma_start(
                out=bs(RB, NC * 512),
                in_=x1d[:, :, w * 512:(w + 1) * 512].rearrange("c p n -> p c n")),
                inc="LX", n=16)
            lxx = em.cnt["LX"]
            em.op("vector", lambda e, w=w: e.tensor_tensor(
                out=row(r_t1, 512), in0=row(r_m2 + w * 512, 512),
                in1=row(r_a2 + w * 512, 512), op=ALU.mult), inc="DV")
            bcast(row(r_a2 + w * 512, 512), of_bc, bf=False)
            bcast(row(r_t1, 512), of_bc + 512, bf=False)
            dv_bc = em.cnt["DV"]
            for dc in range(NC):
                par = dc % 2
                lwo = stream_cols(owT[:, dc * 128:(dc + 1) * 128], 128, par)
                em.w("tensor", "LW", lwo)
                em.w("tensor", "LX", lxy)
                for yc in range(NC):
                    mm(ps[4][:, :], bs(o_wb[par] + yc * 128, 128),
                       bs(RD + yc * 512, 512), yc == 0, yc == NC - 1,
                       inc="PE" if yc == NC - 1 else None)
                wb_free[par] = em.cnt["PE"]
                pe_p = em.cnt["PE"]
                em.w("vector", "PE", pe_p)
                em.w("vector", "LX", lxx)
                em.w("vector", "DV", dv_bc)
                em.op("vector", lambda e, dc=dc: e.tensor_tensor(
                    out=fs(of_tmp, 512), in0=bs(RB + dc * 512, 512), in1=fs(of_bc, 512),
                    op=ALU.mult), inc="DV")
                em.op("vector", lambda e: e.tensor_tensor(
                    out=fs(of_tmp, 512), in0=fs(of_tmp, 512), in1=fs(of_bc + 512, 512),
                    op=ALU.subtract), inc="DV")
                em.w("vector", "ST", em.cnt["ST"])
                em.op("vector", lambda e, dc=dc, w=w: e.tensor_tensor(
                    out=bs(o_kpre, 512), in0=fs(of_tmp, 512),
                    in1=ps[4][:, :], op=ALU.add), inc="DV")
                g2dv = em.cnt["DV"]
                em.w("tensor", "DV", g2dv)
                em.w("sync", "DV", g2dv)
                em.op("sync", lambda e, dc=dc, w=w: e.dma_start(
                    out=g2d[dc, :, w * 512:(w + 1) * 512], in_=bs(o_kpre, 512)),
                    inc="ST", n=16)
            em.w("sync", "DV", em.cnt["DV"])
        em.barrier()

        o_h2 = RF
        for w in range(NW):
            em.op("sync", lambda e, w=w: e.dma_start(
                out=bs(RB, NC * 512),
                in_=g2d[:, :, w * 512:(w + 1) * 512].rearrange("c p n -> p c n")),
                inc="LX", n=16)
            lxg = em.cnt["LX"]
            em.w("scalar", "LX", lxg)
            for c in range(NC):
                em.w("scalar", "PE", em.cnt["PE"])
                em.op("scalar", lambda e, c=c: e.activation(
                    out=bs(o_sq, 512), in_=bs(RB + c * 512, 512),
                    func=AF.Square), inc="SC")
                em.w("tensor", "SC", em.cnt["SC"])
                mm(ps[5][:, :], bs(o_ones, 128), bs(o_sq, 512), c == 0, c == NC - 1,
                   inc="PE")
            em.w("scalar", "PE", em.cnt["PE"])
            em.op("scalar", lambda e: e.activation(
                out=row(r_t1, 512), in_=ps[5][0:1, :], func=AF.Ln,
                bias=np.float32(1e-6), scale=np.float32(1.0 / D)), inc="SC")
            em.op("scalar", lambda e: e.activation(
                out=row(r_t1, 512), in_=row(r_t1, 512), func=AF.Exp,
                scale=np.float32(-0.5)), inc="SC")
            em.w("tensor", "SC", em.cnt["SC"])
            bcast(row(r_t1, 512), o_gb, bf=True)
            em.w("vector", "DV", em.cnt["DV"])
            for c in range(NC):
                em.op("vector", lambda e, c=c, w=w: e.scalar_tensor_tensor(
                    out=bs(o_h2 + c * T + w * 512, 512),
                    in0=bs(RB + c * 512, 512), scalar=fs(of_fnw + c, 1),
                    in1=bs(o_gb, 512), op0=ALU.mult, op1=ALU.mult), inc="DV")
            em.w("sync", "DV", em.cnt["DV"])
        em.barrier()

        o_comb = RB
        for tt in range(NC):
            for c in range(NC):
                mm(ps[4][:, 0:8], bs(o_h2 + c * T + tt * 128, 128),
                   bs(o_gw + c * EXP, EXP), c == 0, c == NC - 1,
                   inc="PE" if c == NC - 1 else None)
            pe_g = em.cnt["PE"]
            em.w("vector", "PE", pe_g)
            g_lg = fs(of_bc, 8)
            g_e = fs(of_bc + 8, 8)
            g_m1 = fs(of_bc + 16, 1)
            g_z = fs(of_bc + 17, 1)
            g_t = fs(of_bc + 18, 8)
            g_m2 = fs(of_bc + 26, 1)
            g_den = fs(of_bc + 27, 1)
            g_keep = fs(of_bc + 28, 8)
            em.op("vector", lambda e: e.tensor_copy(g_lg, ps[4][:, 0:8]), inc="DV")
            em.op("vector", lambda e: e.reduce_max(g_m1, g_lg, axis=AX.X), inc="DV")
            em.op("vector", lambda e: e.tensor_scalar(
                out=g_e, in0=g_lg, scalar1=g_m1, scalar2=None, op0=ALU.subtract), inc="DV")
            em.w("scalar", "DV", em.cnt["DV"])
            em.op("scalar", lambda e: e.activation(out=g_e, in_=g_e, func=AF.Exp), inc="SC")
            em.w("vector", "SC", em.cnt["SC"])
            em.op("vector", lambda e: e.tensor_reduce(
                g_z, g_e, axis=AX.X, op=ALU.add), inc="DV")
            em.op("vector", lambda e: e.reciprocal(g_z, g_z), inc="DV")
            em.op("vector", lambda e: e.tensor_scalar(
                out=g_e, in0=g_e, scalar1=g_z, scalar2=None, op0=ALU.mult), inc="DV")
            em.op("vector", lambda e: e.reduce_max(g_m1, g_e, axis=AX.X), inc="DV")
            em.op("vector", lambda e: e.tensor_scalar(
                out=g_t, in0=g_e, scalar1=g_m1, scalar2=None, op0=ALU.is_lt), inc="DV")
            em.op("vector", lambda e: e.tensor_tensor(
                out=g_t, in0=g_e, in1=g_t, op=ALU.mult), inc="DV")
            em.op("vector", lambda e: e.reduce_max(g_m2, g_t, axis=AX.X), inc="DV")
            em.op("vector", lambda e: e.tensor_tensor(
                out=g_den, in0=g_m1, in1=g_m2, op=ALU.add), inc="DV")
            em.op("vector", lambda e: e.tensor_scalar(
                out=g_den, in0=g_den, scalar1=1e-9, scalar2=None, op0=ALU.add), inc="DV")
            em.op("vector", lambda e: e.reciprocal(g_den, g_den), inc="DV")
            em.op("vector", lambda e: e.tensor_scalar(
                out=g_keep, in0=g_e, scalar1=g_m2, scalar2=None, op0=ALU.is_ge), inc="DV")
            em.op("vector", lambda e: e.tensor_tensor(
                out=g_keep, in0=g_e, in1=g_keep, op=ALU.mult), inc="DV")
            em.op("vector", lambda e: e.tensor_scalar(
                out=g_keep, in0=g_keep, scalar1=g_den, scalar2=None, op0=ALU.mult), inc="DV")
            dv_c = em.cnt["DV"]
            em.w("tensor", "DV", dv_c)
            for grp in range(2):
                em.w("tensor", "DV", em.cnt["DV"])
                em.w("tensor", "SC", em.cnt["SC"])
                for si in range(4):
                    ei = grp * 4 + si
                    em.op("tensor", lambda e, ei=ei, si=si: e.transpose(
                        ps[5][0:1, si * 128:(si + 1) * 128], g_keep[:, ei:ei + 1],
                        fs(of_identf, 128)), inc="PE")
                em.w("vector", "PE", em.cnt["PE"])
                for si in range(4):
                    ei = grp * 4 + si
                    em.op("vector", lambda e, ei=ei, si=si, tt=tt: e.tensor_copy(
                        pb[0:1, o_comb + ei * T + tt * 128:o_comb + ei * T + (tt + 1) * 128],
                        ps[5][0:1, si * 128:(si + 1) * 128]), inc="DV")
        em.barrier()

        o_mwb = [o_wb[0], o_wb[1], o_qh2[0], o_qh2[1]]
        mwb_free = [0, 0, 0, 0]
        o_crow = of_bc
        for ei in range(EXP):
            for w in range(NW):
                em.w("tensor", "DV", em.cnt["DV"])
                em.w("tensor", "SC", em.cnt["SC"])
                mm(ps[6][:, :], pb[0:1, o_ones:o_ones + 128],
                   pb[0:1, o_comb + ei * T + w * 512:o_comb + ei * T + (w + 1) * 512],
                   True, True, inc="PE")
                em.w("vector", "PE", em.cnt["PE"])
                em.op("vector", lambda e, w=w: e.tensor_copy(
                    bs(o_cb + w * 512, 512), ps[6][:, :]), inc="DV")
                em.w("tensor", "DV", em.cnt["DV"])
            dv_cb = em.cnt["DV"]
            for ic in range(11):
                pg = ic % 2
                pu = 2 + ic % 2
                lwg = None
                em.w("gpsimd", "PE", mwb_free[pg])
                em.op("gpsimd", lambda e, o=o_mwb[pg], ei=ei, ic=ic: e.dma_start(
                    out=bs(o, NC * 128),
                    in_=wgT[ei, :, ic * 128:(ic + 1) * 128].rearrange("(c p) n -> p c n", p=128)),
                    inc="LW", n=16, guard=False)
                lwg = em.cnt["LW"]
                em.w("gpsimd", "PE", mwb_free[pu])
                em.op("gpsimd", lambda e, o=o_mwb[pu], ei=ei, ic=ic: e.dma_start(
                    out=bs(o, NC * 128),
                    in_=wuT[ei, :, ic * 128:(ic + 1) * 128].rearrange("(c p) n -> p c n", p=128)),
                    inc="LW", n=16, guard=False)
                lwu = em.cnt["LW"]
                em.w("tensor", "LW", lwu)
                em.w("tensor", "SC", em.cnt["SC"])
                for w in range(NW):
                    for c in range(NC):
                        mm(ps[w][:, :], bs(o_mwb[pg] + c * 128, 128),
                           bs(o_h2 + c * T + w * 512, 512), c == 0, c == NC - 1,
                           inc="PE" if (c == NC - 1 and w == NW - 1) else None)
                mwb_free[pg] = em.cnt["PE"]
                for w in range(NW):
                    pk = ps[4 + (w % 2)]
                    for c in range(NC):
                        mm(pk[:, :], bs(o_mwb[pu] + c * 128, 128),
                           bs(o_h2 + c * T + w * 512, 512), c == 0, c == NC - 1,
                           inc="PE" if c == NC - 1 else None)
                    pe_eu = em.cnt["PE"]
                    if w == NW - 1:
                        mwb_free[pu] = pe_eu
                    em.w("scalar", "PE", pe_eu)
                    em.op("scalar", lambda e, w=w: e.activation(
                        out=bs(o_sg, 512), in_=ps[w][:, :], func=AF.Silu), inc="SC")
                    sc_sg = em.cnt["SC"]
                    em.w("vector", "SC", sc_sg)
                    em.w("vector", "DV", dv_cb)
                    em.w("vector", "ST", em.cnt["ST"])
                    em.op("vector", lambda e, pk=pk: e.tensor_tensor(
                        out=fs(of_tmp, 512), in0=pk[:, :], in1=bs(o_sg, 512),
                        op=ALU.mult), inc="DV")
                    em.op("vector", lambda e, w=w: e.tensor_tensor(
                        out=bs(o_actc + w * 512, 512), in0=fs(of_tmp, 512),
                        in1=bs(o_cb + w * 512, 512), op=ALU.mult), inc="DV")
                    em.w("tensor", "DV", em.cnt["DV"])
                dv_act = em.cnt["DV"]
                em.w("sync", "DV", dv_act)
                em.op("sync", lambda e, ei=ei, ic=ic: e.dma_start(
                    out=actd[ei, ic], in_=bs(o_actc, 4 * 512)), inc="ST", n=16)
        em.barrier()

        o_rw = RF
        o_g2w = RF + 8192
        o_acte = CB
        o_wdb = o_wb
        wdb_free = [0, 0]
        for qtr in range(4):
            em.w("gpsimd", "DV", em.cnt["DV"])
            em.w("gpsimd", "PE", em.cnt["PE"])
            em.op("gpsimd", lambda e, qtr=qtr: e.dma_start(
                out=bs(o_emq, NC * 256),
                in_=emat_i[:, qtr * 256:(qtr + 1) * 256].rearrange(
                    "(c p) n -> p c n", p=128)), inc="LW", n=16)
            lw_e = em.cnt["LW"]
            for w in range(NW):
                em.w("sync", "DV", em.cnt["DV"])
                em.op("sync", lambda e, w=w: e.dma_start(
                    out=bs(o_rw, NC * 512),
                    in_=x1d[:, :, w * 512:(w + 1) * 512].rearrange("c p n -> p c n")),
                    inc="LX", n=16)
                em.op("sync", lambda e, w=w: e.dma_start(
                    out=bs(o_g2w, NC * 512),
                    in_=g2d[:, :, w * 512:(w + 1) * 512].rearrange("c p n -> p c n")),
                    inc="LX", n=16)
                lxr = em.cnt["LX"]
                em.w("vector", "LX", lxr)
                for c in range(NC):
                    em.op("vector", lambda e, c=c: e.tensor_tensor(
                        out=bs(o_rw + c * 512, 512), in0=bs(o_rw + c * 512, 512),
                        in1=bs(o_g2w + c * 512, 512), op=ALU.add), inc="DV")
                dv_r = em.cnt["DV"]
                em.w("tensor", "DV", dv_r)
                em.w("tensor", "LW", lw_e)
                for dq in range(2):
                    pk = ps[dq % 2]
                    for c in range(NC):
                        mm(pk[:, :], bs(o_emq + c * 256 + dq * 128, 128),
                           bs(o_rw + c * 512, 512), c == 0, c == NC - 1,
                           inc="PE" if c == NC - 1 else None)
                    em.w("vector", "PE", em.cnt["PE"])
                    em.op("vector", lambda e, dq=dq, w=w, pk=pk: e.tensor_copy(
                        oacc[:, dq * T + w * 512:dq * T + (w + 1) * 512], pk[:, :]), inc="DV")
                    em.w("tensor", "DV", em.cnt["DV"])
            for ei in range(EXP):
                em.w("sync", "DV", em.cnt["DV"])
                em.op("sync", lambda e, ei=ei: e.dma_start(
                    out=bs(o_acte, 11 * T), in_=actd[ei].rearrange("c p n -> p c n")),
                    inc="LX", n=16)
                lxa = em.cnt["LX"]
                em.w("tensor", "LX", lxa)
                for dq in range(2):
                    dcp = qtr * 2 + dq
                    par = dq % 2
                    em.w("gpsimd", "PE", wdb_free[par])
                    em.op("gpsimd", lambda e, o=o_wdb[par], ei=ei, dcp=dcp: e.dma_start(
                        out=bs(o, 11 * 128),
                        in_=wdTh[ei, :, dcp * 128:(dcp + 1) * 128].rearrange(
                            "(c p) n -> p c n", p=128)), inc="LW", n=16, guard=False)
                    lwd = em.cnt["LW"]
                    em.w("tensor", "LW", lwd)
                    for w in range(NW):
                        pk = ps[4 + (w % 2)]
                        for ic in range(11):
                            mm(pk[:, :], bs(o_wdb[par] + ic * 128, 128),
                               bs(o_acte + ic * T + w * 512, 512), ic == 0, ic == 10,
                               inc="PE" if ic == 10 else None)
                        if w == NW - 1:
                            wdb_free[par] = em.cnt["PE"]
                        em.w("vector", "PE", em.cnt["PE"])
                        em.op("vector", lambda e, dq=dq, w=w, pk=pk: e.tensor_tensor(
                            out=oacc[:, dq * T + w * 512:dq * T + (w + 1) * 512],
                            in0=oacc[:, dq * T + w * 512:dq * T + (w + 1) * 512],
                            in1=pk[:, :], op=ALU.add), inc="DV")
                        em.w("tensor", "DV", em.cnt["DV"])
            em.w("sync", "DV", em.cnt["DV"])
            for dq in range(2):
                dcp = qtr * 2 + dq
                em.op("sync", lambda e, dq=dq, dcp=dcp: e.dma_start(
                    out=outp[dcp * 128:(dcp + 1) * 128, :],
                    in_=oacc[:, dq * T:(dq + 1) * T]), inc="ST", n=16)
            st_h = em.cnt["ST"]
            em.w("vector", "ST", st_h)
        em.w("sync", "ST", em.cnt["ST"])
        em.barrier()

        @blk.tensor
        def _(e):
            for f in em.ops["tensor"]:
                f(e)

        @blk.vector
        def _(e):
            for f in em.ops["vector"]:
                f(e)

        @blk.scalar
        def _(e):
            for f in em.ops["scalar"]:
                f(e)

        @blk.sync
        def _(e):
            for f in em.ops["sync"]:
                f(e)

        @blk.gpsimd
        def _(e):
            for f in em.ops["gpsimd"]:
                f(e)

    return nc


def _rope_tables():
    freqs = 1.0 / (ROPE_THETA ** (np.arange(0, HS, 2, dtype=np.float64) / HS))
    t = np.arange(T, dtype=np.float64)
    f = np.outer(freqs, t)
    ct = np.concatenate([np.cos(f), np.cos(f)], axis=0)
    st = np.concatenate([-np.sin(f), np.sin(f)], axis=0)
    return ct.astype(BF), st.astype(BF)


_NC_CACHE = {}


def kernel(L0_attn_x, L0_attn_Wq_w, L0_attn_Wdkv_w, L0_attn_Wuk_w, L0_attn_Wuv_w,
           L0_attn_Wo_w, L0_ffn_attn_norm_w, L0_ffn_q_w, L0_ffn_k_w, L0_ffn_v_w,
           L0_ffn_o_w, L0_ffn_q_norm_w, L0_ffn_k_norm_w, L0_ffn_ffn_norm_w,
           L0_ffn_gate_w, L0_ffn_wg, L0_ffn_wu, L0_ffn_wd):
    x = np.asarray(L0_attn_x, f32)
    B = x.shape[0]
    ac = np.ascontiguousarray

    wqn = ac(np.asarray(L0_attn_Wq_w, f32))
    wqT = ac(wqn.T)
    wuk = ac(np.asarray(L0_attn_Wuk_w, f32))
    wuv = ac(np.asarray(L0_attn_Wuv_w, f32))
    woT = ac(np.asarray(L0_attn_Wo_w, f32).T)
    wdkvT = ac(np.asarray(L0_attn_Wdkv_w, f32).T)
    qwT = ac(np.asarray(L0_ffn_q_w, f32).T)
    kwT = ac(np.asarray(L0_ffn_k_w, f32).T)
    vwT = ac(np.asarray(L0_ffn_v_w, f32).T)
    owT = ac(np.asarray(L0_ffn_o_w, f32).T)
    gwT = ac(np.asarray(L0_ffn_gate_w, f32).T)
    wgT = ac(np.transpose(np.asarray(L0_ffn_wg, f32), (0, 2, 1)))
    wuT = ac(np.transpose(np.asarray(L0_ffn_wu, f32), (0, 2, 1)))
    wdT = np.transpose(np.asarray(L0_ffn_wd, f32), (0, 2, 1))

    anw = ac(np.asarray(L0_ffn_attn_norm_w, f32).reshape(NC, 128).T)
    fnw = ac(np.asarray(L0_ffn_ffn_norm_w, f32).reshape(NC, 128).T)
    qnw = ac(np.asarray(L0_ffn_q_norm_w, f32).reshape(128, 1))
    knw = ac(np.asarray(L0_ffn_k_norm_w, f32).reshape(128, 1))

    ct, st = _rope_tables()
    jj = np.arange(896)[None, :]
    ii = np.arange(128)[:, None]
    cm = (jj >= ii + 384).astype(BF)
    onesb = np.ones((128, 128), dtype=BF)
    onesf = np.ones((1, 128), dtype=f32)
    identb = np.eye(128, dtype=BF)
    identf = np.eye(128, dtype=f32)

    if "nc" not in _NC_CACHE:
        _NC_CACHE["nc"] = build_nc()
    nc = _NC_CACHE["nc"]

    halves = []
    for half in range(2):
        wdTh = ac(wdT[:, :, half * 1024:(half + 1) * 1024])
        em_np = np.zeros((D, 1024), dtype=BF)
        em_np[half * 1024:(half + 1) * 1024, :] = np.eye(1024, dtype=BF)
        halves.append((wdTh, em_np))

    in_maps = []
    for d in range(8):
        b = d // 2
        wdTh, em_np = halves[d % 2]
        in_maps.append({
            "xfm": ac(x[b].T), "wqn": wqn, "wuk": wuk, "wuv": wuv, "woT": woT,
            "wdkvT": wdkvT, "wqT": wqT, "qwT": qwT, "kwT": kwT, "vwT": vwT,
            "owT": owT, "gwT": gwT, "wgT": wgT, "wuT": wuT, "wdTh": wdTh,
            "emat": em_np, "anw": anw, "fnw": fnw, "qnw": qnw, "knw": knw,
            "ct": ct, "st": st, "cm": cm, "onesb": onesb, "onesf": onesf,
            "identb": identb, "identf": identf,
        })

    res = run_bass_kernel_spmd(nc, in_maps, core_ids=list(range(8)))
    out = np.empty((B, T, D), dtype=f32)
    for b in range(B):
        top = res.results[2 * b]["outp"]
        bot = res.results[2 * b + 1]["outp"]
        out[b] = np.concatenate([top, bot], axis=0).T
    return out
